# version 02
# baseline (speedup 1.0000x reference)
import math
import numpy as np
from contextlib import ExitStack

import concourse.bass as bass
import concourse.mybir as mybir
from concourse.bacc import Bacc
from concourse.bass_utils import run_bass_kernel_spmd
from concourse.alu_op_type import AluOpType as OP

F32 = mybir.dt.float32
BF16 = mybir.dt.bfloat16
I32 = mybir.dt.int32
I16 = mybir.dt.int16
AF = mybir.ActivationFunctionType
AX = mybir.AxisListType

D_MODEL = 1024
D_INNER = 2048
D_STATE = 16
DT_RANK = 64
BATCH = 2
SEQ = 16384
TOK = SEQ // 4
FF = 4096
EXP = 8
HEADS = 4
NSLOTS = 64
CAP = 1024
GCHUNK = 64
GATE_NORM = 8.0

TWO23 = float(2.0 ** 23)
ISQ2 = 0.7071067811865476
HALO = 256
HTOK = TOK + 2 * HALO
PT = 512
NT_IP = HTOK // PT
NTH = HTOK // 128


def _sl(i, n):
    return slice(i * n, (i + 1) * n)


class Seq:

    def __init__(self, nc, ex):
        self.nc = nc
        self.sem_c = ex(nc.semaphore("seq_c"))
        self.sem_d = ex(nc.semaphore("seq_d"))
        self.sem_g = ex(nc.semaphore("seq_g"))
        self.self_sems = {n: ex(nc.semaphore("selfs_" + n))
                          for n in ("vector", "scalar", "tensor", "gpsimd", "sync")}
        self.self_cnt = {n: 0 for n in self.self_sems}
        self.items = []

    def c(self, eng, fn):
        self.items.append((eng, fn, "c"))

    def d(self, fn, n=1):
        fn.n_dma = n
        self.items.append(("sync", fn, "d"))

    def g(self, fn, n=1):
        fn.n_dma = n
        self.items.append(("gpsimd", fn, "g"))

    def emit(self, block):
        runs = []
        for it in self.items:
            if runs and runs[-1][0][0] == it[0] and runs[-1][0][2] == it[2]:
                runs[-1].append(it)
            else:
                runs.append([it])
        cums = []
        cur = {"c": 0, "d": 0, "g": 0}
        for r in runs:
            cums.append(dict(cur))
            kind = r[0][2]
            if kind == "c":
                cur["c"] += 1
            else:
                cur[kind] += 16 * sum(fn.n_dma for (_, fn, _) in r)
        self.final = dict(cur)

        per_engine = {}
        for ri, r in enumerate(runs):
            per_engine.setdefault(r[0][0], []).append(ri)

        engines = {
            "sync": block.sync, "vector": block.vector, "scalar": block.scalar,
            "tensor": block.tensor, "gpsimd": block.gpsimd,
        }
        seq = self

        class _Rec:

            def __init__(self, e, name):
                self._e = e
                self._name = name
                self.last = None

            def __getattr__(self, n):
                a = getattr(self._e, n)
                if callable(a):
                    def w(*args, **kw):
                        r = a(*args, **kw)
                        if hasattr(r, "then_inc"):
                            self.last = r
                        return r
                    return w
                return a

            def edge(self):
                assert self.last is not None
                sem = seq.self_sems[self._name]
                seq.self_cnt[self._name] += 1
                self.last.then_inc(sem, 1)
                self._e.wait_ge(sem, seq.self_cnt[self._name])
                self.last = None

        def emit_engine(eng_name):
            dec = engines[eng_name]

            @dec
            def _(e):
                rec = _Rec(e, eng_name)
                for ri in per_engine.get(eng_name, []):
                    r = runs[ri]
                    tgt = cums[ri]
                    kind = r[0][2]
                    if tgt["c"]:
                        e.wait_ge(seq.sem_c, tgt["c"])
                    if tgt["d"]:
                        e.wait_ge(seq.sem_d, tgt["d"])
                    if tgt["g"]:
                        e.wait_ge(seq.sem_g, tgt["g"])
                    if kind == "c":
                        rec.last = None
                        for fi, (_, fn, _) in enumerate(r):
                            fn(rec)
                            if fi < len(r) - 1 and rec.last is not None:
                                rec.edge()
                        if rec.last is None:
                            e.nop().then_inc(seq.sem_c, 1)
                        else:
                            rec.last.then_inc(seq.sem_c, 1)
                            rec.last = None
                    else:
                        sem = seq.sem_d if kind == "d" else seq.sem_g
                        for (_, fn, _) in r:
                            insts = fn(rec)
                            assert len(insts) == fn.n_dma, (
                                f"dma step declared {fn.n_dma} but emitted {len(insts)}")
                            for inst in insts:
                                inst.then_inc(sem, 16)

        for name in ("sync", "vector", "scalar", "tensor", "gpsimd"):
            if name in per_engine:
                emit_engine(name)

        f = self.final

        @block.sync
        def _(e):
            if f["c"]:
                e.wait_ge(self.sem_c, f["c"])
            if f["d"]:
                e.wait_ge(self.sem_d, f["d"])
            if f["g"]:
                e.wait_ge(self.sem_g, f["g"])



def emit_pwl_steps(seq, ps_bank, b1q_col, pw, out_ap, W, cvt=None):
    B, f1, u1, q, m, z23 = pw["B"], pw["f1"], pw["u1"], pw["q"], pw["m"], pw["z23"]
    f1u = pw["f1u"]
    xi2, e2, y0, y1, fd, fw = pw["xi2"], pw["e2"], pw["y0"], pw["y1"], pw["fd"], pw["fw"]

    C_F1 = 16.0 - TWO23 / 64 - 31.5 / 64 + TWO23
    C_XI = -(TWO23 + 16.0)
    C_U1X = 16.0 - TWO23 / 64

    def s_act(e):
        e.activation(out=B[:, 0:W], in_=ps_bank, func=AF.Identity, scale=cvt[:, 4:5],
                     bias=b1q_col)
    seq.c("scalar", s_act)

    def s_v1(e):
        e.tensor_scalar(out=f1[:, 0:W], in0=B[:, 0:W], scalar1=1.0 / 64,
                        scalar2=C_F1, op0=OP.mult, op1=OP.add)
        e.edge()
        e.tensor_scalar(out=xi2[:, 0:W], in0=f1[:, 0:W], scalar1=C_XI,
                        scalar2=0.25, op0=OP.add, op1=OP.mult)
        e.edge()
        e.tensor_scalar(out=xi2[:, W:2 * W], in0=xi2[:, 0:W], scalar1=0.25,
                        scalar2=None, op0=OP.add, op1=OP.bypass)
    seq.c("vector", s_v1)

    def s_erf(e):
        e.activation(out=e2[:, 0:2 * W], in_=xi2[:, 0:2 * W], func=AF.Erf,
                     scale=cvt[:, 3:4])
    seq.c("scalar", s_erf)

    def s_v2(e):
        e.scalar_tensor_tensor(out=y0[:, 0:W], in0=e2[:, 0:W], scalar=1.0,
                               in1=xi2[:, 0:W], op0=OP.add, op1=OP.mult)
        e.scalar_tensor_tensor(out=y1[:, 0:W], in0=e2[:, W:2 * W], scalar=1.0,
                               in1=xi2[:, W:2 * W], op0=OP.add, op1=OP.mult)
        e.edge()
        e.tensor_tensor(out=fd[:, 0:W], in0=y1[:, 0:W], in1=y0[:, 0:W],
                        op=OP.subtract)
        e.tensor_scalar(out=u1[:, 0:W], in0=B[:, 0:W], scalar1=1.0 / 64,
                        scalar2=C_U1X, op0=OP.mult, op1=OP.add)
        e.tensor_scalar(out=f1u[:, 0:W], in0=f1[:, 0:W], scalar1=TWO23,
                        scalar2=None, op0=OP.subtract, op1=OP.bypass)
        e.edge()
        e.tensor_tensor(out=fw[:, 0:W], in0=u1[:, 0:W], in1=f1u[:, 0:W],
                        op=OP.subtract)
        e.edge()
        e.tensor_tensor(out=fw[:, 0:W], in0=fw[:, 0:W], in1=fd[:, 0:W], op=OP.mult)
        e.edge()
        e.tensor_tensor(out=y0[:, 0:W], in0=y0[:, 0:W], in1=fw[:, 0:W], op=OP.add)
        e.edge()
        e.tensor_scalar(out=q[:, 0:W], in0=y0[:, 0:W], scalar1=128.0,
                        scalar2=TWO23, op0=OP.mult, op1=OP.add)
        e.tensor_scalar(out=m[:, 0:W], in0=B[:, 0:W], scalar1=TWO23 + 1024.0,
                        scalar2=None, op0=OP.is_gt, op1=OP.bypass)
        e.edge()
        e.copy_predicated(out=q[:, 0:W], mask=m[:, 0:W], data=B[:, 0:W])
        e.edge()
        e.tensor_scalar(out=m[:, 0:W], in0=B[:, 0:W], scalar1=TWO23 - 1024.0,
                        scalar2=None, op0=OP.is_lt, op1=OP.bypass)
        e.edge()
        e.copy_predicated(out=q[:, 0:W], mask=m[:, 0:W], data=z23[:, 0:W])
        e.edge()
        e.tensor_scalar(out=out_ap, in0=q[:, 0:W], scalar1=TWO23,
                        scalar2=1.0 / 256, op0=OP.subtract, op1=OP.mult)
    seq.c("vector", s_v2)



def build_ph1():
    nc = Bacc("TRN2", target_bir_lowering=False)

    x_in = nc.dram_tensor("x_halo", [HTOK, D_MODEL], F32, kind="ExternalInput").ap()
    nw = nc.dram_tensor("norm_w", [128, D_MODEL], F32, kind="ExternalInput").ap()
    nb = nc.dram_tensor("norm_b", [128, D_MODEL], F32, kind="ExternalInput").ap()
    win = nc.dram_tensor("win_t", [128, 8 * 2 * D_INNER], F32, kind="ExternalInput").ap()
    convw = nc.dram_tensor("conv_w", [128, 2 * 16 * 4], F32, kind="ExternalInput").ap()
    convb = nc.dram_tensor("conv_b", [128, 2 * 16], F32, kind="ExternalInput").ap()
    xpw = nc.dram_tensor("xpw_t", [128, 16 * 192], F32, kind="ExternalInput").ap()
    w1 = nc.dram_tensor("w1", [128, 8 * FF], F32, kind="ExternalInput").ap()
    b1 = nc.dram_tensor("b1", [128, FF // 128], F32, kind="ExternalInput").ap()
    w2 = nc.dram_tensor("w2", [128, 32 * D_MODEL], F32, kind="ExternalInput").ap()
    b2 = nc.dram_tensor("b2", [128, D_MODEL // 128], F32, kind="ExternalInput").ap()
    ident = nc.dram_tensor("ident", [128, 128], BF16, kind="ExternalInput").ap()
    cvec = nc.dram_tensor("cvec", [128, 8], F32, kind="ExternalInput").ap()

    z_out = nc.dram_tensor("z", [D_INNER, TOK], BF16, kind="ExternalOutput").ap()
    uf_out = nc.dram_tensor("u_f", [D_INNER, TOK], BF16, kind="ExternalOutput").ap()
    ur_out = nc.dram_tensor("u_r", [D_INNER, TOK], BF16, kind="ExternalOutput").ap()
    xdbl_out = nc.dram_tensor("xdbl", [2 * 96, TOK], F32, kind="ExternalOutput").ap()
    g_out = nc.dram_tensor("g", [D_MODEL, TOK], BF16, kind="ExternalOutput").ap()

    xzx_h = nc.dram_tensor("xzx_h", [D_INNER, HTOK], BF16, kind="Internal").ap()
    xnf_h = nc.dram_tensor("xnf_h", [128, 8 * TOK], BF16, kind="Internal").ap()
    xg_h = nc.dram_tensor("xg_h", [128, 32 * TOK], BF16, kind="Internal").ap()

    with ExitStack() as st:
        ex = st.enter_context
        wbig = ex(nc.sbuf_tensor("wbig", [128, 8 * 2 * D_INNER], BF16))
        nwt = ex(nc.sbuf_tensor("nwt", [128, D_MODEL], F32))
        nbt = ex(nc.sbuf_tensor("nbt", [128, D_MODEL], F32))
        idt = ex(nc.sbuf_tensor("idt", [128, 128], BF16))
        convwt = ex(nc.sbuf_tensor("convwt", [128, 2 * 16 * 4], F32))
        convbt = ex(nc.sbuf_tensor("convbt", [128, 2 * 16], F32))
        xpwt = ex(nc.sbuf_tensor("xpwt", [128, 16 * 192], BF16))
        b1t = ex(nc.sbuf_tensor("b1t", [128, FF // 128], F32))
        b1qt = ex(nc.sbuf_tensor("b1qt", [128, FF // 128], F32))
        b2qt = ex(nc.sbuf_tensor("b2qt", [128, D_MODEL // 128], F32))
        b2t = ex(nc.sbuf_tensor("b2t", [128, D_MODEL // 128], F32))
        cvt = ex(nc.sbuf_tensor("cvt", [128, 8], F32))

        xt = ex(nc.sbuf_tensor("xt", [128, D_MODEL], F32))
        stt = ex(nc.sbuf_tensor("stt", [128, 8], F32))
        xnt = ex(nc.sbuf_tensor("xnt", [128, D_MODEL], F32))
        sq = ex(nc.sbuf_tensor("sq", [128, D_MODEL], F32))
        hint = ex(nc.sbuf_tensor("hint", [128, D_MODEL], BF16))
        xnbt = ex(nc.sbuf_tensor("xnbt", [128, D_MODEL], BF16))
        hinF = ex(nc.sbuf_tensor("hinF", [128, 8 * PT], BF16))
        xnF = ex(nc.sbuf_tensor("xnF", [128, 8 * 128], BF16))
        ar32 = ex(nc.sbuf_tensor("ar32", [128, 32 * PT], BF16))
        psT = ex(nc.psum_tensor("psT", [128, 1024], BF16))
        psA = ex(nc.psum_tensor("psA", [128, 2 * PT], F32))

        cxt = ex(nc.sbuf_tensor("cxt", [128, PT + 8], BF16))
        cac = ex(nc.sbuf_tensor("cac", [128, PT], F32))
        uo = ex(nc.sbuf_tensor("uo", [128, 16 * PT], BF16))
        psX = ex(nc.psum_tensor("psX", [96, 2 * PT], F32))
        xdt = ex(nc.sbuf_tensor("xdt", [96, 2 * PT], F32))

        fxn = ex(nc.sbuf_tensor("fxn", [128, 8 * PT], BF16))
        fgo = ex(nc.sbuf_tensor("fgo", [128, 8 * PT], BF16))
        pw = {k: ex(nc.sbuf_tensor("pw_" + k, [128, PT], F32))
              for k in ("B", "f1", "f1u", "u1", "q", "z23", "y0", "y1", "fd", "fw")}
        pw["m"] = ex(nc.sbuf_tensor("pw_m", [128, PT], mybir.dt.uint8))
        pw["xi2"] = ex(nc.sbuf_tensor("pw_xi2", [128, 2 * PT], F32))
        pw["e2"] = ex(nc.sbuf_tensor("pw_e2", [128, 2 * PT], F32))

        block = ex(nc.Block())
        seq = Seq(nc, ex)

        seq.g(lambda e: [e.dma_start(out=wbig[:], in_=win),
                         e.dma_start(out=xpwt[:], in_=xpw),
                         e.dma_start(out=idt[:], in_=ident)], 3)
        seq.d(lambda e: [e.dma_start(out=nwt[:], in_=nw),
                         e.dma_start(out=nbt[:], in_=nb),
                         e.dma_start(out=convwt[:], in_=convw),
                         e.dma_start(out=convbt[:], in_=convb),
                         e.dma_start(out=b1t[:], in_=b1),
                         e.dma_start(out=b2t[:], in_=b2),
                         e.dma_start(out=cvt[:], in_=cvec)], 7)

        def prep(e):
            e.tensor_scalar(out=b1qt[:], in0=b1t[:], scalar1=256.0, scalar2=TWO23,
                            op0=OP.mult, op1=OP.add)
            e.tensor_scalar(out=b2qt[:], in0=b2t[:], scalar1=256.0, scalar2=TWO23,
                            op0=OP.mult, op1=OP.add)
            e.memset(pw["z23"][:], TWO23)
        seq.c("vector", prep)

        def ln_group(i):
            seq.d(lambda e, i=i: [e.dma_start(out=xt[:], in_=x_in[_sl(i, 128), :])])

            def v1(e):
                e.tensor_reduce(out=stt[:, 0:1], in_=xt[:], axis=AX.X, op=OP.add)
                e.edge()
                e.tensor_scalar(out=stt[:, 0:1], in0=stt[:, 0:1],
                                scalar1=1.0 / D_MODEL, scalar2=None,
                                op0=OP.mult, op1=OP.bypass)
            seq.c("vector", v1)

            def a1(e):
                e.activation(out=sq[:], in_=xt[:], func=AF.Square, scale=cvt[:, 2:3],
                             bias=stt[:, 0:1], accum_out=stt[:, 1:2])
                e.edge()
                e.activation(out=stt[:, 2:3], in_=stt[:, 1:2], func=AF.Sqrt,
                             scale=cvt[:, 0:1], bias=cvt[:, 1:2])
            seq.c("scalar", a1)

            def v2(e):
                e.reciprocal(out=stt[:, 3:4], in_=stt[:, 2:3])
                e.edge()
                e.tensor_scalar(out=xnt[:], in0=xt[:], scalar1=stt[:, 0:1],
                                scalar2=stt[:, 3:4], op0=OP.subtract, op1=OP.mult)
                e.edge()
                e.tensor_reduce(out=stt[:, 4:5], in_=xnt[:], axis=AX.X, op=OP.add)
                e.edge()
                e.tensor_scalar(out=stt[:, 4:5], in0=stt[:, 4:5],
                                scalar1=1.0 / D_MODEL, scalar2=None,
                                op0=OP.mult, op1=OP.bypass)
            seq.c("vector", v2)

            def a2(e):
                e.activation(out=sq[:], in_=xnt[:], func=AF.Square, scale=cvt[:, 2:3],
                             bias=stt[:, 4:5], accum_out=stt[:, 5:6])
                e.edge()
                e.activation(out=stt[:, 6:7], in_=stt[:, 5:6], func=AF.Sqrt,
                             scale=cvt[:, 0:1], bias=cvt[:, 1:2])
            seq.c("scalar", a2)

            core_grp = 2 <= i < NTH - 2

            def v3(e, core_grp=core_grp):
                e.reciprocal(out=stt[:, 7:8], in_=stt[:, 6:7])
                e.edge()
                e.tensor_scalar(out=xnt[:], in0=xnt[:], scalar1=stt[:, 4:5],
                                scalar2=stt[:, 7:8], op0=OP.subtract, op1=OP.mult)
                e.edge()
                e.tensor_tensor(out=hint[:], in0=xnt[:], in1=nwt[:], op=OP.mult)
                e.edge()
                e.tensor_tensor(out=hint[:], in0=hint[:], in1=nbt[:], op=OP.add)
                if core_grp:
                    e.tensor_copy(out=xnbt[:], in_=xnt[:])
            seq.c("vector", v3)

            def t1(e):
                for j in range(8):
                    e.transpose(psT[:, _sl(j, 128)], hint[:, _sl(j, 128)], idt[:])
            seq.c("tensor", t1)

            def ev1(e, i=i):
                for j in range(8):
                    e.copy(out=hinF[:, j * PT + (i % 4) * 128:][:, 0:128],
                           in_=psT[:, _sl(j, 128)])
            seq.c("scalar", ev1)

            if core_grp:
                def t2(e):
                    for j in range(8):
                        e.transpose(psT[:, _sl(j, 128)], xnbt[:, _sl(j, 128)], idt[:])
                seq.c("tensor", t2)

                def ev2(e):
                    for j in range(8):
                        e.copy(out=xnF[:, _sl(j, 128)], in_=psT[:, _sl(j, 128)])
                seq.c("scalar", ev2)

                seq.d(lambda e, i=i: [e.dma_start(
                    out=xnf_h.rearrange("p (a t) -> p a t", a=8)[:, :, _sl(i - 2, 128)],
                    in_=xnF[:].rearrange("p (a t) -> p a t", a=8))])

        def inproj_tile(ti):
            for et in range(32):
                bank = et % 2

                def mm1(e, et=et, bank=bank):
                    for j in range(8):
                        e.matmul(psA[:, _sl(bank, PT)],
                                 wbig[:, j * (2 * D_INNER) + et * 128:][:, 0:128],
                                 hinF[:, _sl(j, PT)],
                                 start=(j == 0), stop=(j == 7))
                seq.c("tensor", mm1)

                def ev(e, et=et, bank=bank):
                    e.copy(out=ar32[:, _sl(et, PT)], in_=psA[:, _sl(bank, PT)])
                seq.c("scalar", ev)

            lo = max(HALO, ti * PT)
            hi = min(HALO + TOK, (ti + 1) * PT)

            def stx(e, ti=ti, lo=lo, hi=hi):
                insts = [e.dma_start(
                    out=xzx_h.rearrange("(a p) t -> p a t", p=128)[:, :, _sl(ti, PT)],
                    in_=ar32[:, 0:16 * PT].rearrange("p (a t) -> p a t", a=16))]
                if hi > lo:
                    insts.append(e.dma_start(
                        out=z_out.rearrange("(a p) t -> p a t", p=128)[:, :, lo - HALO:hi - HALO],
                        in_=ar32[:, 16 * PT:].rearrange("p (a t) -> p a t", a=16)[
                            :, :, lo - ti * PT:hi - ti * PT]))
                return insts
            seq.d(stx, 2 if hi > lo else 1)

        for ti in range(NT_IP):
            for i in range(4 * ti, 4 * ti + 4):
                ln_group(i)
            inproj_tile(ti)

        for ti in range(8):
            for d in range(2):
                for et in range(16):
                    seq.d(lambda e, et=et, ti=ti: [e.dma_start(
                        out=cxt[:],
                        in_=xzx_h[_sl(et, 128), HALO + ti * PT - 4:
                                  HALO + (ti + 1) * PT + 4])])

                    def cv(e, d=d, et=et):
                        wcol = convwt[:, (d * 16 + et) * 4:]
                        sh = [1 + k if d == 0 else 7 - k for k in range(4)]
                        e.tensor_scalar(out=cac[:], in0=cxt[:, sh[0]:sh[0] + PT],
                                        scalar1=wcol[:, 0:1], scalar2=None,
                                        op0=OP.mult, op1=OP.bypass)
                        for k in range(1, 4):
                            e.edge()
                            e.scalar_tensor_tensor(out=cac[:],
                                                   in0=cxt[:, sh[k]:sh[k] + PT],
                                                   scalar=wcol[:, k:k + 1],
                                                   in1=cac[:],
                                                   op0=OP.mult, op1=OP.add)
                    seq.c("vector", cv)

                    def sl_(e, d=d, et=et):
                        e.activation(out=uo[:, _sl(et, PT)], in_=cac[:],
                                     func=AF.Silu,
                                     bias=convbt[:, d * 16 + et:][:, 0:1])
                    seq.c("scalar", sl_)

                def xp(e, d=d):
                    for j in range(16):
                        e.matmul(psX[:, _sl(d, PT)],
                                 xpwt[:, j * 192 + d * 96:][:, 0:96],
                                 uo[:, _sl(j, PT)],
                                 start=(j == 0), stop=(j == 15))
                seq.c("tensor", xp)

                def xpe(e, d=d):
                    e.copy(out=xdt[:, _sl(d, PT)], in_=psX[:, _sl(d, PT)])
                seq.c("scalar", xpe)

                def stu(e, d=d, ti=ti):
                    out = uf_out if d == 0 else ur_out
                    return [e.dma_start(
                        out=out.rearrange("(a p) t -> p a t", p=128)[:, :, _sl(ti, PT)],
                        in_=uo[:].rearrange("p (a t) -> p a t", a=16))]
                seq.d(stu)

            seq.d(lambda e, ti=ti: [e.dma_start(
                out=xdbl_out[:, _sl(ti, PT)].rearrange("(a p) t -> p a t", p=96),
                in_=xdt[:].rearrange("p (a t) -> p a t", a=2))])

        seq.g(lambda e: [e.dma_start(out=wbig[:, 0:8 * FF], in_=w1)], 1)
        for ti in range(8):
            seq.d(lambda e, ti=ti: [e.dma_start(
                out=fxn[:],
                in_=xnf_h.rearrange("p (a t) -> p a t", a=8)[:, :, _sl(ti, PT)])])

            for ft in range(32):
                bank = ft % 2

                def mmf(e, ft=ft, bank=bank):
                    for j in range(8):
                        e.matmul(psA[:, _sl(bank, PT)],
                                 wbig[:, j * FF + ft * 128:][:, 0:128],
                                 fxn[:, _sl(j, PT)],
                                 start=(j == 0), stop=(j == 7))
                seq.c("tensor", mmf)
                emit_pwl_steps(seq, psA[:, _sl(bank, PT)], b1qt[:, ft:ft + 1],
                               pw, ar32[:, _sl(ft, PT)], PT, cvt=cvt)

            seq.d(lambda e, ti=ti: [e.dma_start(
                out=xg_h.rearrange("p (a t) -> p a t", a=32)[:, :, _sl(ti, PT)],
                in_=ar32[:].rearrange("p (a t) -> p a t", a=32))])

        seq.g(lambda e: [e.dma_start(out=wbig[:, 0:32 * D_MODEL], in_=w2)], 1)
        for ti in range(8):
            seq.d(lambda e, ti=ti: [e.dma_start(
                out=ar32[:],
                in_=xg_h.rearrange("p (a t) -> p a t", a=32)[:, :, _sl(ti, PT)])])

            for dt_ in range(8):
                bank = dt_ % 2

                def mm2(e, dt_=dt_, bank=bank):
                    for j in range(32):
                        e.matmul(psA[:, _sl(bank, PT)],
                                 wbig[:, j * D_MODEL + dt_ * 128:][:, 0:128],
                                 ar32[:, _sl(j, PT)],
                                 start=(j == 0), stop=(j == 31))
                seq.c("tensor", mm2)

                def ev2(e, dt_=dt_, bank=bank):
                    e.activation(out=pw["B"][:], in_=psA[:, _sl(bank, PT)],
                                 func=AF.Identity, scale=cvt[:, 4:5],
                                 bias=b2qt[:, dt_:dt_ + 1])
                    e.edge()
                    e.activation(out=fgo[:, _sl(dt_, PT)], in_=pw["B"][:],
                                 func=AF.Silu, scale=cvt[:, 5:6],
                                 bias=cvt[:, 6:7])
                seq.c("scalar", ev2)

            seq.d(lambda e, ti=ti: [e.dma_start(
                out=g_out.rearrange("(a p) t -> p a t", p=128)[:, :, _sl(ti, PT)],
                in_=fgo[:].rearrange("p (a t) -> p a t", a=8))])

        seq.emit(block)
    return nc




TC2 = 2048
NSC = SEQ // TC2
SUB = 1024


def build_ph2():
    nc = Bacc("TRN2", target_bir_lowering=False)
    u_in = nc.dram_tensor("u2", [2, 4, 128, SEQ], BF16, kind="ExternalInput").ap()
    xd_in = nc.dram_tensor("xdbl2", [2, 96, SEQ], F32, kind="ExternalInput").ap()
    dtw_in = nc.dram_tensor("dtw", [2, 64, 512], F32, kind="ExternalInput").ap()
    dtb_in = nc.dram_tensor("dtb", [2, 128, 4], F32, kind="ExternalInput").ap()
    al_in = nc.dram_tensor("alog", [2, 128, 64], F32, kind="ExternalInput").ap()
    sel_in = nc.dram_tensor("sel32", [32, 32 * 128], BF16, kind="ExternalInput").ap()
    y_out = nc.dram_tensor("ysc", [2, 4, 128, SEQ], BF16, kind="ExternalOutput").ap()

    with ExitStack() as st:
        ex = st.enter_context
        dtwt = ex(nc.sbuf_tensor("dtwt", [64, 2 * 512], BF16))
        dtbt = ex(nc.sbuf_tensor("dtbt", [128, 8], F32))
        alt = ex(nc.sbuf_tensor("alt", [128, 128], F32))
        att = ex(nc.sbuf_tensor("att", [128, 128], F32))
        selt = ex(nc.sbuf_tensor("selt", [32, 32 * 128], BF16))
        ut = ex(nc.sbuf_tensor("ut", [128, 4 * TC2], BF16))
        xdt2 = ex(nc.sbuf_tensor("xdt2", [96, TC2], F32))
        xdb2 = ex(nc.sbuf_tensor("xdb2", [96, TC2], BF16))
        bcr = ex(nc.sbuf_tensor("bcr", [32, TC2], BF16))
        dtt = ex(nc.sbuf_tensor("dtt", [128, 4 * TC2], F32))
        wt2 = ex(nc.sbuf_tensor("wt2", [128, 4 * TC2], BF16))
        Bbt = ex(nc.sbuf_tensor("Bbt", [128, SUB], BF16))
        Cbt = ex(nc.sbuf_tensor("Cbt", [128, SUB], BF16))
        at2 = ex(nc.sbuf_tensor("at2", [128, SUB], F32))
        bt2 = ex(nc.sbuf_tensor("bt2", [128, SUB], BF16))
        ht2 = ex(nc.sbuf_tensor("ht2", [128, SUB], BF16))
        hc2 = ex(nc.sbuf_tensor("hc2", [128, SUB], BF16))
        tree = ex(nc.sbuf_tensor("tree", [128, 16 * SUB], BF16))
        yt2 = ex(nc.sbuf_tensor("yt2", [128, 4 * TC2], BF16))
        stv = ex(nc.sbuf_tensor("stv", [128, 128], F32))
        psB = ex(nc.psum_tensor("psB", [128, SUB], F32))
        psC = ex(nc.psum_tensor("psC", [128, SUB], F32))
        psD = ex(nc.psum_tensor("psD", [128, 512], F32))
        sp1t = ex(nc.sbuf_tensor("sp1t", [128, 512], F32))
        sp2t = ex(nc.sbuf_tensor("sp2t", [128, 512], F32))
        sp3t = ex(nc.sbuf_tensor("sp3t", [128, 512], F32))

        block = ex(nc.Block())
        seq = Seq(nc, ex)

        seq.g(lambda e: [
            e.dma_start(out=dtwt[:, 0:512], in_=dtw_in[0]),
            e.dma_start(out=dtwt[:, 512:1024], in_=dtw_in[1]),
            e.dma_start(out=selt[:], in_=sel_in)], 3)
        seq.d(lambda e: [
            e.dma_start(out=dtbt[:, 0:4], in_=dtb_in[0]),
            e.dma_start(out=dtbt[:, 4:8], in_=dtb_in[1]),
            e.dma_start(out=alt[:, 0:64], in_=al_in[0]),
            e.dma_start(out=alt[:, 64:128], in_=al_in[1])], 4)

        def prep_a(e):
            e.activation(out=att[:], in_=alt[:], func=AF.Exp)
        seq.c("scalar", prep_a)

        def prep_v(e):
            e.tensor_scalar(out=att[:], in0=att[:], scalar1=-1.0, scalar2=None,
                            op0=OP.mult, op1=OP.bypass)
            e.memset(stv[:], 0.0)
        seq.c("vector", prep_v)

        for d in range(2):
            for sc in range(NSC):
                c0 = sc * TC2
                seq.d(lambda e, d=d, c0=c0: [
                    e.dma_start(out=ut[:].rearrange("p (a t) -> p a t", a=4),
                                in_=u_in[d, :, :, c0:c0 + TC2].rearrange("a p t -> p a t")),
                    e.dma_start(out=xdt2[:], in_=xd_in[d, :, c0:c0 + TC2]),
                ], 2)

                def cast_xd(e):
                    e.tensor_copy(out=xdb2[:], in_=xdt2[:])
                seq.c("vector", cast_xd)

                seq.d(lambda e: [e.dma_start(out=bcr[:], in_=xdb2[64:96, :])])

                for ds in range(4):
                    for hf in range(4):
                        def mm1(e, d=d, ds=ds, hf=hf):
                            e.matmul(psD[:], dtwt[:, d * 512 + ds * 128:][:, 0:128],
                                     xdb2[0:64, _sl(hf, 512)],
                                     start=True, stop=True)
                        seq.c("tensor", mm1)

                        def sp_a(e, d=d, ds=ds, hf=hf):
                            e.activation(out=sp1t[:], in_=psD[:], func=AF.Identity,
                                         bias=dtbt[:, d * 4 + ds:][:, 0:1])
                        seq.c("scalar", sp_a)

                        def sp_b(e):
                            e.tensor_scalar(out=sp2t[:], in0=sp1t[:], scalar1=-1.0,
                                            scalar2=None, op0=OP.mult, op1=OP.bypass)
                            e.edge()
                            e.tensor_tensor(out=sp2t[:], in0=sp1t[:], in1=sp2t[:],
                                            op=OP.min)
                        seq.c("vector", sp_b)

                        def sp_c(e):
                            e.activation(out=sp3t[:], in_=sp1t[:], func=AF.Relu)
                            e.activation(out=sp2t[:], in_=sp2t[:], func=AF.Exp)
                            e.edge()
                            e.activation(out=sp2t[:], in_=sp2t[:], func=AF.Ln,
                                         bias=1.0)
                        seq.c("scalar", sp_c)

                        def sp_d(e, ds=ds, hf=hf):
                            e.tensor_tensor(
                                out=dtt[:, _sl(ds, TC2)][:, _sl(hf, 512)],
                                in0=sp3t[:], in1=sp2t[:], op=OP.add)
                        seq.c("vector", sp_d)

                def mkw(e):
                    for ds in range(4):
                        e.tensor_tensor(out=wt2[:, _sl(ds, TC2)],
                                        in0=dtt[:, _sl(ds, TC2)],
                                        in1=ut[:, _sl(ds, TC2)], op=OP.mult)
                seq.c("vector", mkw)

                for q in range(2):
                    occ = [[False] * 4 for _ in range(4)]

                    for n in range(16):
                        def bld(e, n=n, q=q):
                            for hf in range(2):
                                e.matmul(psB[:, _sl(hf, 512)],
                                         selt[:, _sl(n, 128)],
                                         bcr[0:32, _sl(q, SUB)][:, _sl(hf, 512)],
                                         start=True, stop=True)
                                e.matmul(psC[:, _sl(hf, 512)],
                                         selt[:, _sl(16 + n, 128)],
                                         bcr[0:32, _sl(q, SUB)][:, _sl(hf, 512)],
                                         start=True, stop=True)
                        seq.c("tensor", bld)

                        def evac(e):
                            e.activation(out=Bbt[:], in_=psB[:], func=AF.Identity)
                            e.activation(out=Cbt[:], in_=psC[:], func=AF.Identity)
                        seq.c("scalar", evac)

                        for ds in range(4):
                            def s_a(e, d=d, ds=ds, n=n, q=q):
                                e.activation(out=at2[:],
                                             in_=dtt[:, _sl(ds, TC2)][:, _sl(q, SUB)],
                                             func=AF.Exp,
                                             scale=att[:, d * 64 + ds * 16 + n:][:, 0:1])
                            seq.c("scalar", s_a)

                            occ_ds = occ[ds]

                            def s_vec(e, d=d, ds=ds, n=n, q=q, occ_ds=occ_ds):
                                col = d * 64 + ds * 16 + n
                                e.tensor_tensor(out=bt2[:],
                                                in0=wt2[:, _sl(ds, TC2)][:, _sl(q, SUB)],
                                                in1=Bbt[:], op=OP.mult)
                                e.edge()
                                e.tensor_tensor_scan(out=ht2[:], data0=at2[:],
                                                     data1=bt2[:],
                                                     initial=stv[:, col:col + 1],
                                                     op0=OP.mult, op1=OP.add)
                                e.edge()
                                e.tensor_copy(out=stv[:, col:col + 1],
                                              in_=ht2[:, SUB - 1:SUB])
                                if n % 2 == 0:
                                    e.tensor_tensor(out=tree[:, _sl(4 * ds, SUB)],
                                                    in0=ht2[:], in1=Cbt[:], op=OP.mult)
                                    occ_ds[0] = True
                                else:
                                    e.tensor_tensor(out=hc2[:], in0=ht2[:],
                                                    in1=Cbt[:], op=OP.mult)
                                    k = 0
                                    while k < 4 and occ_ds[k]:
                                        k += 1
                                    for L in range(k):
                                        e.edge()
                                        last = (L == k - 1)
                                        if last and k == 4:
                                            dst = yt2[:, _sl(ds, TC2)][:, _sl(q, SUB)]
                                        elif last:
                                            dst = tree[:, _sl(4 * ds + k, SUB)]
                                        else:
                                            dst = hc2[:]
                                        e.tensor_tensor(out=dst,
                                                        in0=tree[:, _sl(4 * ds + L, SUB)],
                                                        in1=hc2[:], op=OP.add)
                                        occ_ds[L] = False
                                    if k < 4:
                                        occ_ds[k] = True
                            seq.c("vector", s_vec)

                def sty(e, d=d, c0=c0):
                    return [e.dma_start(
                        out=y_out[d, :, :, c0:c0 + TC2].rearrange("a p t -> p a t"),
                        in_=yt2[:].rearrange("p (a t) -> p a t", a=4))]
                seq.d(sty)

        seq.emit(block)
    return nc



PT3 = 512
NT3 = TOK // PT3


def build_ph3():
    nc = Bacc("TRN2", target_bir_lowering=False)
    x_in = nc.dram_tensor("x3", [TOK, D_MODEL], F32, kind="ExternalInput").ap()
    z_in = nc.dram_tensor("z3", [D_INNER, TOK], BF16, kind="ExternalInput").ap()
    uf_in = nc.dram_tensor("uf3", [D_INNER, TOK], BF16, kind="ExternalInput").ap()
    ur_in = nc.dram_tensor("ur3", [D_INNER, TOK], BF16, kind="ExternalInput").ap()
    yf_in = nc.dram_tensor("yf3", [D_INNER, TOK], BF16, kind="ExternalInput").ap()
    yr_in = nc.dram_tensor("yr3", [D_INNER, TOK], BF16, kind="ExternalInput").ap()
    gt_in = nc.dram_tensor("gt3", [TOK, D_MODEL], BF16, kind="ExternalInput").ap()
    d2_in = nc.dram_tensor("D2", [128, 32], F32, kind="ExternalInput").ap()
    wo_in = nc.dram_tensor("wout", [128, 16 * D_MODEL], F32, kind="ExternalInput").ap()
    wfiv_in = nc.dram_tensor("wfiv", [128, 3 * 8 * D_MODEL], F32, kind="ExternalInput").ap()
    wg_in = nc.dram_tensor("wg", [128, 8 * 8], F32, kind="ExternalInput").ap()
    nw2_in = nc.dram_tensor("nw2", [128, 8], F32, kind="ExternalInput").ap()
    ident = nc.dram_tensor("ident", [128, 128], BF16, kind="ExternalInput").ap()
    cvec = nc.dram_tensor("cvec", [128, 8], F32, kind="ExternalInput").ap()

    x1_out = nc.dram_tensor("x1", [TOK, D_MODEL], F32, kind="ExternalOutput").ap()
    yloc_out = nc.dram_tensor("yloc", [D_MODEL, TOK], BF16, kind="ExternalOutput").ap()
    cprod_out = nc.dram_tensor("cprod", [D_MODEL, TOK], BF16, kind="ExternalOutput").ap()
    lst_out = nc.dram_tensor("lstates", [128, 16], F32, kind="ExternalOutput").ap()
    xh2_out = nc.dram_tensor("xhat2", [TOK, D_MODEL], BF16, kind="ExternalOutput").ap()
    srw_out = nc.dram_tensor("selrw", [TOK, 4], F32, kind="ExternalOutput").ap()
    me_out = nc.dram_tensor("me8", [TOK, 8], F32, kind="ExternalOutput").ap()

    with ExitStack() as st:
        ex = st.enter_context
        wot = ex(nc.sbuf_tensor("wot", [128, 16 * D_MODEL], BF16))
        wfivt = ex(nc.sbuf_tensor("wfivt", [128, 8 * D_MODEL], BF16))
        wgt = ex(nc.sbuf_tensor("wgt", [128, 8 * 8], F32))
        wgtf = ex(nc.sbuf_tensor("wgtf", [128, 8 * 8], BF16))
        nw2t = ex(nc.sbuf_tensor("nw2t", [128, 8], F32))
        d2t = ex(nc.sbuf_tensor("d2t", [128, 32], F32))
        idt3 = ex(nc.sbuf_tensor("idt3", [128, 128], BF16))
        cvt3 = ex(nc.sbuf_tensor("cvt3", [128, 8], F32))
        iota8 = ex(nc.sbuf_tensor("iota8", [128, 8], F32))

        ydt = ex(nc.sbuf_tensor("ydt", [128, 16 * PT3], BF16))
        zdt = ex(nc.sbuf_tensor("zdt", [128, 16 * PT3], BF16))
        tdt = ex(nc.sbuf_tensor("tdt", [128, 16 * PT3], BF16))

        xt3 = ex(nc.sbuf_tensor("xt3", [128, 4 * D_MODEL], F32))
        stt3 = ex(nc.sbuf_tensor("stt3", [128, 4 * 8], F32))
        xnt3 = ex(nc.sbuf_tensor("xnt3", [128, 4 * D_MODEL], F32))
        sq3 = ex(nc.sbuf_tensor("sq3", [128, D_MODEL], F32))
        gT = ex(nc.sbuf_tensor("gT", [128, 4 * D_MODEL], BF16))
        x1t = ex(nc.sbuf_tensor("x1t", [128, 4 * D_MODEL], F32))
        xn2t = ex(nc.sbuf_tensor("xn2t", [128, 4 * D_MODEL], F32))
        st3 = xn2t
        xh2t = ex(nc.sbuf_tensor("xh2t", [128, 4 * D_MODEL], BF16))
        xn2b = ex(nc.sbuf_tensor("xn2b", [128, D_MODEL], BF16))
        xn2F = ex(nc.sbuf_tensor("xn2F", [128, 8 * PT3], BF16))
        lgt = ex(nc.sbuf_tensor("lgt", [128, 8], F32))
        mx8 = ex(nc.sbuf_tensor("mx8", [128, 8], F32))
        idx8 = ex(nc.sbuf_tensor("idx8", [128, 8], mybir.dt.uint32))
        idxf = ex(nc.sbuf_tensor("idxf", [128, 2], F32))
        srwt = ex(nc.sbuf_tensor("srwt", [128, 4 * 4], F32))
        met = ex(nc.sbuf_tensor("met", [128, 4 * 8], F32))
        m8a = ex(nc.sbuf_tensor("m8a", [128, 8], F32))
        m8b = ex(nc.sbuf_tensor("m8b", [128, 8], F32))

        fvt = tdt[:, 0:8 * PT3].bitcast(BF16)
        wct = tdt[:, 8 * PT3:16 * PT3].bitcast(BF16)
        ylt = zdt[:, 0:8 * PT3].bitcast(BF16)
        cpt = zdt[:, 8 * PT3:16 * PT3].bitcast(BF16)
        zet = ex(nc.sbuf_tensor("zet", [128, PT3], BF16))
        lstt = ex(nc.sbuf_tensor("lstt", [128, 16], F32))
        psO = ex(nc.psum_tensor("psO", [128, 2 * 512], F32))
        psT3 = ex(nc.psum_tensor("psT3", [128, 1024], BF16))
        psL = ex(nc.psum_tensor("psL", [128, 2 * 512], F32))
        psR = ex(nc.psum_tensor("psR", [128, 8], F32))

        block = ex(nc.Block())
        seq = Seq(nc, ex)

        seq.g(lambda e: [e.dma_start(out=wot[:], in_=wo_in),
                         e.dma_start(out=idt3[:], in_=ident)], 2)
        seq.d(lambda e: [e.dma_start(out=d2t[:], in_=d2_in),
                         e.dma_start(out=wgt[:], in_=wg_in),
                         e.dma_start(out=nw2t[:], in_=nw2_in),
                         e.dma_start(out=cvt3[:], in_=cvec)], 4)

        def prep3(e):
            for k in range(8):
                e.tensor_scalar(out=wgtf[:, _sl(k, 8)], in0=wgt[:, _sl(k, 8)],
                                scalar1=nw2t[:, k:k + 1], scalar2=None,
                                op0=OP.mult, op1=OP.bypass)
            e.memset(zet[:], 0.0)
            e.memset(lstt[:, 0:8], 0.0)
            e.memset(lstt[:, 8:16], 1.0)
            for k in range(8):
                e.memset(iota8[:, k:k + 1], float(k))
        seq.c("vector", prep3)

        def emit_ln(src, dst, sb, rs_slot=3):
            def v1(e):
                e.tensor_reduce(out=sb[:, 0:1], in_=src, axis=AX.X, op=OP.add)
                e.edge()
                e.tensor_scalar(out=sb[:, 0:1], in0=sb[:, 0:1],
                                scalar1=1.0 / D_MODEL, scalar2=None,
                                op0=OP.mult, op1=OP.bypass)
            seq.c("vector", v1)

            def a1(e):
                e.activation(out=sq3[:], in_=src, func=AF.Square, scale=cvt3[:, 2:3],
                             bias=sb[:, 0:1], accum_out=sb[:, 1:2])
                e.edge()
                e.activation(out=sb[:, 2:3], in_=sb[:, 1:2], func=AF.Sqrt,
                             scale=cvt3[:, 0:1], bias=cvt3[:, 1:2])
            seq.c("scalar", a1)

            def v2(e):
                e.reciprocal(out=sb[:, rs_slot:rs_slot + 1], in_=sb[:, 2:3])
                e.edge()
                e.tensor_scalar(out=dst, in0=src,
                                scalar1=sb[:, 0:1], scalar2=sb[:, rs_slot:rs_slot + 1],
                                op0=OP.subtract, op1=OP.mult)
            seq.c("vector", v2)

        for ti in range(NT3):
            seq.d(lambda e, ti=ti: [
                e.dma_start(out=ydt[:].rearrange("p (a t) -> p a t", a=16),
                            in_=yf_in.rearrange("(a p) t -> p a t", p=128)[:, :, _sl(ti, PT3)]),
                e.dma_start(out=tdt[:].rearrange("p (a t) -> p a t", a=16),
                            in_=uf_in.rearrange("(a p) t -> p a t", p=128)[:, :, _sl(ti, PT3)]),
            ], 2)

            def ypre_f(e):
                for et in range(16):
                    e.scalar_tensor_tensor(out=ydt[:, _sl(et, PT3)],
                                           in0=tdt[:, _sl(et, PT3)],
                                           scalar=d2t[:, et:et + 1],
                                           in1=ydt[:, _sl(et, PT3)],
                                           op0=OP.mult, op1=OP.add)
            seq.c("vector", ypre_f)

            seq.d(lambda e, ti=ti: [
                e.dma_start(out=zdt[:].rearrange("p (a t) -> p a t", a=16),
                            in_=yr_in.rearrange("(a p) t -> p a t", p=128)[:, :, _sl(ti, PT3)]),
                e.dma_start(out=tdt[:].rearrange("p (a t) -> p a t", a=16),
                            in_=ur_in.rearrange("(a p) t -> p a t", p=128)[:, :, _sl(ti, PT3)]),
            ], 2)

            def ypre_r(e):
                for et in range(16):
                    e.scalar_tensor_tensor(out=zdt[:, _sl(et, PT3)],
                                           in0=tdt[:, _sl(et, PT3)],
                                           scalar=d2t[:, 16 + et:17 + et],
                                           in1=zdt[:, _sl(et, PT3)],
                                           op0=OP.mult, op1=OP.add)
                e.edge()
                for et in range(16):
                    e.tensor_tensor(out=ydt[:, _sl(et, PT3)],
                                    in0=ydt[:, _sl(et, PT3)],
                                    in1=zdt[:, _sl(et, PT3)], op=OP.add)
            seq.c("vector", ypre_r)

            seq.d(lambda e, ti=ti: [
                e.dma_start(out=zdt[:].rearrange("p (a t) -> p a t", a=16),
                            in_=z_in.rearrange("(a p) t -> p a t", p=128)[:, :, _sl(ti, PT3)]),
            ])

            def gate(e):
                for et in range(16):
                    e.activation(out=tdt[:, _sl(et, PT3)], in_=zdt[:, _sl(et, PT3)],
                                 func=AF.Silu)
            seq.c("scalar", gate)

            def gate2(e):
                for et in range(16):
                    e.tensor_tensor(out=ydt[:, _sl(et, PT3)],
                                    in0=ydt[:, _sl(et, PT3)],
                                    in1=tdt[:, _sl(et, PT3)], op=OP.mult)
            seq.c("vector", gate2)

            seq.d(lambda e, ti=ti: [
                e.dma_start(out=xt3[:].rearrange("p (a d) -> p a d", a=4),
                            in_=x_in[_sl(ti, PT3), :].rearrange("(a p) d -> p a d", p=128)),
                e.dma_start(out=gT[:].rearrange("p (a d) -> p a d", a=4),
                            in_=gt_in[_sl(ti, PT3), :].rearrange("(a p) d -> p a d", p=128)),
            ], 2)
            for grp in range(4):
                emit_ln(xt3[:, _sl(grp, D_MODEL)], xnt3[:, _sl(grp, D_MODEL)],
                        stt3[:, _sl(grp, 8)])

            for grp in range(4):
                def mmo(e, grp=grp):
                    for half in range(2):
                        for k in range(16):
                            e.matmul(psO[:, _sl(half, 512)],
                                     ydt[:, k * PT3 + grp * 128:][:, 0:128],
                                     wot[:, k * D_MODEL + half * 512:][:, 0:512],
                                     start=(k == 0), stop=(k == 15))
                seq.c("tensor", mmo)

                def souts(e, grp=grp):
                    for half in range(2):
                        e.tensor_tensor(
                            out=st3[:, grp * D_MODEL + half * 512:][:, 0:512],
                            in0=psO[:, _sl(half, 512)],
                            in1=xnt3[:, grp * D_MODEL + half * 512:][:, 0:512],
                            op=OP.add)
                    e.edge()
                    e.tensor_tensor(out=x1t[:, _sl(grp, D_MODEL)],
                                    in0=st3[:, _sl(grp, D_MODEL)],
                                    in1=gT[:, _sl(grp, D_MODEL)], op=OP.mult)
                    e.edge()
                    e.tensor_tensor(out=x1t[:, _sl(grp, D_MODEL)],
                                    in0=x1t[:, _sl(grp, D_MODEL)],
                                    in1=xt3[:, _sl(grp, D_MODEL)], op=OP.add)
                seq.c("vector", souts)

            seq.d(lambda e, ti=ti: [
                e.dma_start(out=x1_out[_sl(ti, PT3), :].rearrange("(a p) d -> p a d", p=128),
                            in_=x1t[:].rearrange("p (a d) -> p a d", a=4))])

            for grp in range(4):
                emit_ln(x1t[:, _sl(grp, D_MODEL)], xn2t[:, _sl(grp, D_MODEL)],
                        stt3[:, _sl(grp, 8)])

                def rms2(e, grp=grp):
                    sb = stt3[:, _sl(grp, 8)]
                    e.activation(out=sq3[:], in_=xn2t[:, _sl(grp, D_MODEL)],
                                 func=AF.Square, accum_out=sb[:, 4:5])
                    e.edge()
                    e.activation(out=sb[:, 5:6], in_=sb[:, 4:5], func=AF.Sqrt,
                                 scale=cvt3[:, 0:1], bias=cvt3[:, 1:2])
                seq.c("scalar", rms2)

                def rms3(e, grp=grp):
                    sb = stt3[:, _sl(grp, 8)]
                    e.reciprocal(out=sb[:, 6:7], in_=sb[:, 5:6])
                    e.edge()
                    e.tensor_scalar(out=xh2t[:, _sl(grp, D_MODEL)],
                                    in0=xn2t[:, _sl(grp, D_MODEL)],
                                    scalar1=sb[:, 6:7], scalar2=None,
                                    op0=OP.mult, op1=OP.bypass)
                    e.tensor_copy(out=xn2b[:], in_=xn2t[:, _sl(grp, D_MODEL)])
                seq.c("vector", rms3)

                def t2F(e, grp=grp):
                    for j in range(8):
                        e.transpose(psT3[:, _sl(j, 128)], xn2b[:, _sl(j, 128)], idt3[:])
                seq.c("tensor", t2F)

                def ev2F(e, grp=grp):
                    for j in range(8):
                        e.copy(out=xn2F[:, j * PT3 + grp * 128:][:, 0:128],
                               in_=psT3[:, _sl(j, 128)])
                seq.c("scalar", ev2F)

            seq.d(lambda e, ti=ti: [
                e.dma_start(out=xh2_out[_sl(ti, PT3), :].rearrange("(a p) d -> p a d", p=128),
                            in_=xh2t[:].rearrange("p (a d) -> p a d", a=4))])

            for grp in range(4):
                def mmr(e, grp=grp):
                    for k in range(8):
                        e.matmul(psR[:], xn2F[:, k * PT3 + grp * 128:][:, 0:128],
                                 wgtf[:, _sl(k, 8)],
                                 start=(k == 0), stop=(k == 7))
                seq.c("tensor", mmr)

                def rte(e, grp=grp):
                    sb = stt3[:, _sl(grp, 8)]
                    e.tensor_scalar(out=lgt[:], in0=psR[:], scalar1=sb[:, 6:7],
                                    scalar2=None, op0=OP.mult, op1=OP.bypass)
                    e.edge()
                    e.tensor_reduce(out=sb[:, 7:8], in_=lgt[:], axis=AX.X, op=OP.max)
                    e.edge()
                    e.tensor_scalar(out=lgt[:], in0=lgt[:], scalar1=sb[:, 7:8],
                                    scalar2=None, op0=OP.subtract, op1=OP.bypass)
                seq.c("vector", rte)

                def rte2(e):
                    e.activation(out=lgt[:], in_=lgt[:], func=AF.Exp)
                seq.c("scalar", rte2)

                def rte3(e, grp=grp):
                    sb = stt3[:, _sl(grp, 8)]
                    e.tensor_reduce(out=sb[:, 7:8], in_=lgt[:], axis=AX.X, op=OP.add)
                    e.edge()
                    e.reciprocal(out=sb[:, 7:8], in_=sb[:, 7:8])
                    e.edge()
                    e.tensor_scalar(out=lgt[:], in0=lgt[:], scalar1=sb[:, 7:8],
                                    scalar2=None, op0=OP.mult, op1=OP.bypass)
                    e.edge()
                    e.max(out=mx8[:], in_=lgt[:])
                    e.edge()
                    e.max_index(out=idx8[:], in_max=mx8[:], in_values=lgt[:])
                    e.edge()
                    e.tensor_copy(out=idxf[:, 0:2], in_=idx8[:, 0:2])
                    e.edge()
                    srw = srwt[:, _sl(grp, 4)]
                    e.tensor_copy(out=srw[:, 0:2], in_=idxf[:, 0:2])
                    e.tensor_tensor(out=sb[:, 7:8], in0=mx8[:, 0:1], in1=mx8[:, 1:2],
                                    op=OP.add)
                    e.edge()
                    e.reciprocal(out=sb[:, 7:8], in_=sb[:, 7:8])
                    e.edge()
                    e.tensor_scalar(out=srw[:, 2:3], in0=mx8[:, 0:1],
                                    scalar1=sb[:, 7:8], scalar2=None,
                                    op0=OP.mult, op1=OP.bypass)
                    e.tensor_scalar(out=srw[:, 3:4], in0=mx8[:, 1:2],
                                    scalar1=sb[:, 7:8], scalar2=None,
                                    op0=OP.mult, op1=OP.bypass)
                    e.tensor_scalar(out=m8a[:], in0=iota8[:], scalar1=idxf[:, 0:1],
                                    scalar2=None, op0=OP.is_equal, op1=OP.bypass)
                    e.tensor_scalar(out=m8b[:], in0=iota8[:], scalar1=idxf[:, 1:2],
                                    scalar2=None, op0=OP.is_equal, op1=OP.bypass)
                    e.edge()
                    e.tensor_tensor(out=met[:, _sl(grp, 8)], in0=m8a[:], in1=m8b[:],
                                    op=OP.add)
                seq.c("vector", rte3)

            seq.d(lambda e, ti=ti: [
                e.dma_start(out=srw_out[_sl(ti, PT3), :].rearrange("(a p) c -> p a c", p=128),
                            in_=srwt[:].rearrange("p (a c) -> p a c", a=4)),
                e.dma_start(out=me_out[_sl(ti, PT3), :].rearrange("(a p) c -> p a c", p=128),
                            in_=met[:].rearrange("p (a c) -> p a c", a=4)),
            ], 2)

            for w_i, name in ((0, "f"), (1, "i"), (2, "v")):
                seq.g(lambda e, w_i=w_i: [e.dma_start(
                    out=wfivt[:], in_=wfiv_in[:, w_i * 8 * D_MODEL:(w_i + 1) * 8 * D_MODEL])])
                for nt in range(8):
                    def mml(e, w_i=w_i, nt=nt):
                        bank = nt % 2
                        for k in range(8):
                            e.matmul(psL[:, _sl(bank, 512)],
                                     wfivt[:, k * D_MODEL + nt * 128:][:, 0:128],
                                     xn2F[:, _sl(k, PT3)],
                                     start=(k == 0), stop=(k == 7))
                    seq.c("tensor", mml)

                    def acl(e, w_i=w_i, nt=nt):
                        bank = nt % 2
                        if w_i == 0:
                            e.activation(out=fvt[:, _sl(nt, PT3)],
                                         in_=psL[:, _sl(bank, 512)], func=AF.Tanh)
                        elif w_i == 1:
                            e.activation(out=wct[:, _sl(nt, PT3)],
                                         in_=psL[:, _sl(bank, 512)], func=AF.Sigmoid)
                        else:
                            e.activation(out=ylt[:, _sl(nt, PT3)],
                                         in_=psL[:, _sl(bank, 512)], func=AF.Silu)
                    seq.c("scalar", acl)

            def wcmul(e):
                for nt in range(8):
                    e.tensor_tensor(out=wct[:, _sl(nt, PT3)], in0=wct[:, _sl(nt, PT3)],
                                    in1=ylt[:, _sl(nt, PT3)], op=OP.mult)
            seq.c("vector", wcmul)

            def scans(e):
                for nt in range(8):
                    e.tensor_tensor_scan(out=ylt[:, _sl(nt, PT3)],
                                         data0=fvt[:, _sl(nt, PT3)],
                                         data1=wct[:, _sl(nt, PT3)],
                                         initial=lstt[:, nt:nt + 1],
                                         op0=OP.mult, op1=OP.add)
                    e.edge()
                    e.tensor_copy(out=lstt[:, nt:nt + 1],
                                  in_=ylt[:, nt * PT3 + PT3 - 1:][:, 0:1])
                    e.tensor_tensor_scan(out=cpt[:, _sl(nt, PT3)],
                                         data0=fvt[:, _sl(nt, PT3)],
                                         data1=zet[:],
                                         initial=lstt[:, 8 + nt:9 + nt],
                                         op0=OP.mult, op1=OP.add)
                    e.edge()
                    e.tensor_copy(out=lstt[:, 8 + nt:9 + nt],
                                  in_=cpt[:, nt * PT3 + PT3 - 1:][:, 0:1])
            seq.c("vector", scans)

            seq.d(lambda e, ti=ti: [
                e.dma_start(out=yloc_out.rearrange("(a p) t -> p a t", p=128)[:, :, _sl(ti, PT3)],
                            in_=ylt[:].rearrange("p (a t) -> p a t", a=8)),
                e.dma_start(out=cprod_out.rearrange("(a p) t -> p a t", p=128)[:, :, _sl(ti, PT3)],
                            in_=cpt[:].rearrange("p (a t) -> p a t", a=8)),
            ], 2)

        seq.d(lambda e: [e.dma_start(out=lst_out, in_=lstt[:])])
        seq.emit(block)
    return nc



NCH = CAP // GCHUNK
TRUNC4 = None


def build_ph4():
    nc = Bacc("TRN2", target_bir_lowering=False)
    xh_in = nc.dram_tensor("xh4", [SEQ, D_MODEL], BF16, kind="ExternalInput").ap()
    me_in = nc.dram_tensor("me2", [2, SEQ], F32, kind="ExternalInput").ap()
    wq_in = nc.dram_tensor("wq4", [128, 8 * D_MODEL], F32, kind="ExternalInput").ap()
    wk_in = nc.dram_tensor("wk4", [2, 128, 8 * D_MODEL], F32, kind="ExternalInput").ap()
    wv_in = nc.dram_tensor("wv4", [2, 128, 8 * D_MODEL], F32, kind="ExternalInput").ap()
    wf_in = nc.dram_tensor("wf4", [2, 128, 8 * 256], F32, kind="ExternalInput").ap()
    nw_in = nc.dram_tensor("nw4", [128, 8], F32, kind="ExternalInput").ap()
    tri_in = nc.dram_tensor("triu64", [64, 64], BF16, kind="ExternalInput").ap()
    trif_in = nc.dram_tensor("triuf4", [64, 256], F32, kind="ExternalInput").ap()
    id_in = nc.dram_tensor("ident", [128, 128], BF16, kind="ExternalInput").ap()

    rb_in = nc.dram_tensor("rb16", [16, 1], F32, kind="ExternalInput").ap()
    ob_out = nc.dram_tensor("oblk", [2, CAP, D_MODEL], BF16, kind="ExternalOutput").ap()
    rk_out = nc.dram_tensor("rk2", [2, SEQ], F32, kind="ExternalOutput").ap()
    tot_out = nc.dram_tensor("tot2", [1, 4], F32, kind="ExternalOutput").ap()
    widx_h = nc.dram_tensor("widx_h", [2, CAP], I16, kind="Internal").ap()
    rowh = nc.dram_tensor("rowh", [1, 32], F32, kind="Internal").ap()

    with ExitStack() as st:
        ex = st.enter_context
        wstg = ex(nc.sbuf_tensor("wstg", [128, 4096], F32))
        wqt = ex(nc.sbuf_tensor("wqt", [128, 8 * D_MODEL], BF16))
        wkt = ex(nc.sbuf_tensor("wkt", [128, 8 * D_MODEL], BF16))
        wvt4 = ex(nc.sbuf_tensor("wvt4", [128, 8 * D_MODEL], BF16))
        wft = ex(nc.sbuf_tensor("wft", [128, 8 * 256], BF16))
        nwt = ex(nc.sbuf_tensor("nwt", [128, 8], F32))
        trit = ex(nc.sbuf_tensor("trit", [64, 64], BF16))
        trif = ex(nc.sbuf_tensor("trif", [64, 256], F32))
        idt4 = ex(nc.sbuf_tensor("idt4", [128, 128], BF16))
        onesr = ex(nc.sbuf_tensor("onesr", [1, 128], BF16))
        scal4 = ex(nc.sbuf_tensor("scal4", [1, 8], F32))
        tott = ex(nc.sbuf_tensor("tott", [1, 4], F32))

        me16 = ex(nc.sbuf_tensor("me16", [16, CAP], F32))
        rc16 = ex(nc.sbuf_tensor("rc16", [16, CAP], F32))
        rank16 = ex(nc.sbuf_tensor("rank16", [16, CAP], F32))
        pos16 = ex(nc.sbuf_tensor("pos16", [16, CAP], F32))
        one16 = ex(nc.sbuf_tensor("one16", [16, CAP], F32))
        si16b = ex(nc.sbuf_tensor("si16b", [16, CAP], I16))
        wtmp16 = ex(nc.sbuf_tensor("wtmp16", [16, CAP], I16))
        di16 = ex(nc.sbuf_tensor("di16", [16, CAP], I16))
        wtm2 = ex(nc.sbuf_tensor("wtm2", [8, CAP], F32))
        rb16 = ex(nc.sbuf_tensor("rb16t", [16, 1], F32))
        scal16 = ex(nc.sbuf_tensor("scal16", [16, 2], F32))
        ecol = ex(nc.sbuf_tensor("ecol", [16, 1], F32))
        pcol = ex(nc.sbuf_tensor("pcol", [16, 1], F32))
        o16r = ex(nc.sbuf_tensor("o16r", [1, 16], F32))
        rowr = ex(nc.sbuf_tensor("rowr", [1, 16], F32))
        incl = ex(nc.sbuf_tensor("incl", [1, 16], F32))
        excl = ex(nc.sbuf_tensor("excl", [1, 16], F32))
        e2r = ex(nc.sbuf_tensor("e2r", [1, 16], F32))
        vmkb = ex(nc.sbuf_tensor("vmkb", [1, CAP], BF16))
        wsci = ex(nc.sbuf_tensor("wsci", [1, CAP], I16))
        widxw = ex(nc.sbuf_tensor("widxw", [128, CAP // 16], I16))
        vmkB = ex(nc.sbuf_tensor("vmkB", [128, CAP], BF16))

        xeFa = ex(nc.sbuf_tensor("xeFa", [128, 8 * 512], BF16))
        xeFb = ex(nc.sbuf_tensor("xeFb", [128, 8 * 512], BF16))
        qF = ex(nc.sbuf_tensor("qF", [128, 8 * CAP], BF16))
        ktC = ex(nc.sbuf_tensor("ktC", [64, 1024], BF16))
        vtC = ex(nc.sbuf_tensor("vtC", [64, 1024], BF16))
        fe = ex(nc.sbuf_tensor("fe", [64, 256], F32))
        nfe = ex(nc.sbuf_tensor("nfe", [64, 256], F32))
        ab4 = ex(nc.sbuf_tensor("ab4", [64, 256], F32))
        gP = ex(nc.sbuf_tensor("gP", [64, 256], F32))
        gPb = ex(nc.sbuf_tensor("gPb", [64, 256], BF16))
        stP = ex(nc.sbuf_tensor("stP", [64, 256], F32))
        stPb = ex(nc.sbuf_tensor("stPb", [64, 256], BF16))
        dec4 = ex(nc.sbuf_tensor("dec4", [64, 256], F32))
        exm = ex(nc.sbuf_tensor("exm", [64, 256], F32))
        exmT = ex(nc.sbuf_tensor("exmT", [64, 256], BF16))
        GcT4 = ex(nc.sbuf_tensor("GcT4", [64, 256], F32))
        stT4 = ex(nc.sbuf_tensor("stT4", [64, 256], BF16))
        sinT4 = ex(nc.sbuf_tensor("sinT4", [64, 256], BF16))
        sin4 = ex(nc.sbuf_tensor("sin4", [64, 256], BF16))
        kF8 = ex(nc.sbuf_tensor("kF8", [128, 512], BF16))
        hkT8 = ex(nc.sbuf_tensor("hkT8", [128, 512], BF16))
        hkcast = ex(nc.sbuf_tensor("hkcast", [64, 1024], BF16))
        hvcast = ex(nc.sbuf_tensor("hvcast", [64, 1024], BF16))
        At4 = ex(nc.sbuf_tensor("At4", [64, 256], BF16))
        ok4 = ex(nc.sbuf_tensor("ok4", [64, 256], F32))
        qve = ex(nc.sbuf_tensor("qve", [64, 256], F32))
        qv32 = ex(nc.sbuf_tensor("qv32", [64, 256], F32))
        qv4 = ex(nc.sbuf_tensor("qv4", [64, 256], BF16))
        qvT4 = ex(nc.sbuf_tensor("qvT4", [64, 256], BF16))
        TmT4 = ex(nc.sbuf_tensor("TmT4", [64, 256], BF16))
        scrR = ex(nc.sbuf_tensor("scrR", [64, 256], F32))
        exmR = ex(nc.sbuf_tensor("exmR", [64, 256], BF16))
        srmT4 = ex(nc.sbuf_tensor("srmT4", [64, 256], BF16))
        srm4 = ex(nc.sbuf_tensor("srm4", [64, 256], BF16))
        ov4 = ex(nc.sbuf_tensor("ov4", [64, D_MODEL], BF16))
        hkm = ex(nc.sbuf_tensor("hkm", [64, 1024], F32))
        hvm = ex(nc.sbuf_tensor("hvm", [64, 1024], F32))
        mx4 = ex(nc.sbuf_tensor("mx4", [64, 4], F32))
        sm4 = ex(nc.sbuf_tensor("sm4", [64, 4], F32))
        sm4r = ex(nc.sbuf_tensor("sm4r", [64, 4], F32))
        glT4 = ex(nc.sbuf_tensor("glT4", [64, 4], F32))

        psP2 = ex(nc.psum_tensor("psP2", [128, 1024], F32))
        psQ = ex(nc.psum_tensor("psQ", [128, 512], F32))
        psG = ex(nc.psum_tensor("psG", [64, 256], F32))
        psG2 = ex(nc.psum_tensor("psG2", [64, 256], F32))
        psT4 = ex(nc.psum_tensor("psT4", [128, 1024], BF16))
        psOK = ex(nc.psum_tensor("psOK", [64, 256], F32))
        psU = ex(nc.psum_tensor("psU", [64, 256], F32))

        block = ex(nc.Block())
        seq = Seq(nc, ex)

        seq.g(lambda e: [e.dma_start(out=idt4[:], in_=id_in)], 1)
        seq.d(lambda e: [e.dma_start(out=trit[:], in_=tri_in),
                         e.dma_start(out=trif[:], in_=trif_in),
                         e.dma_start(out=nwt[:], in_=nw_in)], 3)

        def prep4(e):
            e.memset(tott[:], 0.0)
            e.memset(onesr[:], 1.0)
            e.memset(one16[:], 1.0)
            e.memset(o16r[:], 1.0)
            e.memset(scal16[:, 0:1], 0.0)
            e.memset(scal16[:, 1:2], 1.0)
        seq.c("vector", prep4)
        seq.d(lambda e: [e.dma_start(out=rb16[:], in_=rb_in)])

        def fold_w(src_ap, dstt):
            for half in range(2):
                seq.d(lambda e, half=half: [e.dma_start(
                    out=wstg[:], in_=src_ap[:, half * 4096:(half + 1) * 4096])])

                def fw(e, half=half, dstt=dstt):
                    for k in range(4):
                        kk = half * 4 + k
                        e.tensor_scalar(out=dstt[:, _sl(kk, D_MODEL)],
                                        in0=wstg[:, _sl(k, D_MODEL)],
                                        scalar1=nwt[:, kk:kk + 1], scalar2=None,
                                        op0=OP.mult, op1=OP.bypass)
                seq.c("vector", fw)

        fold_w(wq_in, wqt)

        for gi in range(2):
            seq.d(lambda e, gi=gi: [e.dma_start(
                out=me16[:], in_=me_in[gi].rearrange("(p c) -> p c", p=16))])

            def rka(e):
                e.tensor_tensor_scan(out=rc16[:], data0=one16[:], data1=me16[:],
                                     initial=0.0, op0=OP.mult, op1=OP.add)
                e.edge()
                e.tensor_copy(out=ecol[:], in_=rc16[:, CAP - 1:CAP])
            seq.c("vector", rka)
            seq.d(lambda e: [e.dma_start(
                out=rowh[0, 0:16].rearrange("(p c) -> p c", c=1), in_=ecol[:])])
            seq.c("scalar", lambda e: e.nop())
            seq.d(lambda e: [e.dma_start(out=rowr[:], in_=rowh[0:1, 0:16])])

            def rkb(e, gi=gi):
                e.tensor_tensor_scan(out=incl[:], data0=o16r[:], data1=rowr[:],
                                     initial=0.0, op0=OP.mult, op1=OP.add)
                e.edge()
                e.tensor_tensor(out=excl[:], in0=incl[:], in1=rowr[:],
                                op=OP.subtract)
                e.tensor_copy(out=tott[:, gi:gi + 1], in_=incl[:, 15:16])
                e.edge()
                e.tensor_scalar(out=e2r[:], in0=excl[:], scalar1=incl[:, 15:16],
                                scalar2=float(CAP - 1),
                                op0=OP.subtract, op1=OP.add)
            seq.c("vector", rkb)
            seq.d(lambda e: [e.dma_start(out=rowh[0:1, 0:16], in_=excl[:]),
                             e.dma_start(out=rowh[0:1, 16:32], in_=e2r[:])], 2)
            seq.c("scalar", lambda e: e.nop())
            seq.d(lambda e: [
                e.dma_start(out=ecol[:],
                            in_=rowh[0, 0:16].rearrange("(p c) -> p c", c=1)),
                e.dma_start(out=pcol[:],
                            in_=rowh[0, 16:32].rearrange("(p c) -> p c", c=1)),
            ], 2)

            def rkc(e):
                e.tensor_scalar(out=rank16[:], in0=rc16[:], scalar1=ecol[:],
                                scalar2=None, op0=OP.add, op1=OP.bypass)
                e.tensor_scalar(out=pos16[:], in0=rc16[:], scalar1=pcol[:],
                                scalar2=None, op0=OP.add, op1=OP.bypass)
                e.edge()
                e.scalar_tensor_tensor(out=me16[:], in0=pos16[:],
                                       scalar=scal16[:, 0:1], in1=me16[:],
                                       op0=OP.is_ge, op1=OP.mult)
                e.edge()
                e.scalar_tensor_tensor(out=pos16[:], in0=pos16[:],
                                       scalar=scal16[:, 1:2], in1=me16[:],
                                       op0=OP.add, op1=OP.mult)
                e.edge()
                e.tensor_scalar(out=rc16[:], in0=pos16[:], scalar1=-1.0,
                                scalar2=None, op0=OP.add, op1=OP.bypass)
                e.edge()
                e.tensor_copy(out=si16b[:], in_=rc16[:])
                e.tensor_tensor_scan(out=pos16[:], data0=one16[:], data1=one16[:],
                                     initial=rb16[:], op0=OP.mult, op1=OP.add)
                e.edge()
                e.tensor_copy(out=di16[:], in_=pos16[:])
            seq.c("vector", rkc)
            seq.d(lambda e, gi=gi: [e.dma_start(
                out=rk_out[gi].rearrange("(p c) -> p c", p=16), in_=rank16[:])])

            def sct(e):
                e.local_scatter(out_ap=wtmp16[:], data_ap=di16[:],
                                idxs_ap=si16b[:], channels=16,
                                num_elems=CAP, num_idxs=CAP)
            seq.c("gpsimd", sct)

            def wcv(e):
                e.tensor_copy(out=rank16[:], in_=wtmp16[:])
            seq.c("vector", wcv)
            for width in (8, 4, 2, 1):
                seq.d(lambda e, width=width: [e.dma_start(
                    out=wtm2[0:width, :], in_=rank16[width:2 * width, :])])

                def mrg(e, width=width):
                    e.tensor_tensor(out=rank16[0:width, :], in0=rank16[0:width, :],
                                    in1=wtm2[0:width, :], op=OP.max)
                seq.c("vector", mrg)

            def widx(e):
                e.tensor_scalar(out=wtm2[0:1, :], in0=rank16[0:1, :], scalar1=0.5,
                                scalar2=None, op0=OP.is_ge, op1=OP.bypass)
                e.edge()
                e.tensor_copy(out=vmkb[:], in_=wtm2[0:1, :])
                e.tensor_scalar(out=rank16[0:1, :], in0=rank16[0:1, :],
                                scalar1=-1.0, scalar2=None,
                                op0=OP.add, op1=OP.bypass)
                e.edge()
                e.tensor_scalar(out=rank16[0:1, :], in0=rank16[0:1, :],
                                scalar1=0.0, scalar2=None,
                                op0=OP.max, op1=OP.bypass)
                e.edge()
                e.tensor_copy(out=wsci[:], in_=rank16[0:1, :])
            seq.c("vector", widx)

            seq.d(lambda e, gi=gi: [e.dma_start(out=widx_h[gi:gi + 1, :], in_=wsci[:])])
            seq.c("scalar", lambda e: e.nop())
            seq.d(lambda e, gi=gi: [
                e.dma_start(out=widxw[16 * k:16 * (k + 1), :],
                            in_=widx_h[gi].rearrange("(c p) -> p c", p=16),
                            allow_slow_non_contiguous=True)
                for k in range(8)], 8)

            for half in range(2):
                def bmm(e, half=half):
                    e.matmul(psQ[:], onesr[:], vmkb[:, _sl(half, 512)],
                             start=True, stop=True)
                seq.c("tensor", bmm)

                def bev(e, half=half):
                    e.copy(out=vmkB[:, _sl(half, 512)], in_=psQ[:])
                seq.c("scalar", bev)

            def gat(e):
                return [
                    e.dma_gather(
                        out_ap=xeFa[:].rearrange("p (a t) -> p a t", a=8),
                        in_ap=xh_in, idxs_ap=widxw[:, 0:32], num_idxs=512,
                        num_idxs_reg=512, elem_size=D_MODEL, transpose=True),
                    e.dma_gather(
                        out_ap=xeFb[:].rearrange("p (a t) -> p a t", a=8),
                        in_ap=xh_in, idxs_ap=widxw[:, 32:64], num_idxs=512,
                        num_idxs_reg=512, elem_size=D_MODEL, transpose=True),
                ]
            seq.g(gat, 2)

            def msk(e):
                for k in range(8):
                    e.tensor_tensor(out=xeFa[:, _sl(k, 512)],
                                    in0=xeFa[:, _sl(k, 512)],
                                    in1=vmkB[:, 0:512], op=OP.mult)
                    e.tensor_tensor(out=xeFb[:, _sl(k, 512)],
                                    in0=xeFb[:, _sl(k, 512)],
                                    in1=vmkB[:, 512:1024], op=OP.mult)
            seq.c("vector", msk)

            fold_w(wk_in[gi], wkt)
            fold_w(wv_in[gi], wvt4)
            seq.d(lambda e, gi=gi: [e.dma_start(out=wstg[:, 0:2048], in_=wf_in[gi])])

            def foldf(e):
                for k in range(8):
                    e.tensor_scalar(out=wft[:, _sl(k, 256)], in0=wstg[:, _sl(k, 256)],
                                    scalar1=nwt[:, k:k + 1], scalar2=None,
                                    op0=OP.mult, op1=OP.bypass)
                e.memset(hkm[:], 0.0)
                e.memset(hvm[:], 0.0)
            seq.c("vector", foldf)

            for s in range(8):
                for half in range(2):
                    def qmm(e, s=s, half=half):
                        xh_ = xeFa if half == 0 else xeFb
                        for k in range(8):
                            e.matmul(psQ[:], wqt[:, k * D_MODEL + s * 128:][:, 0:128],
                                     xh_[:, _sl(k, 512)],
                                     start=(k == 0), stop=(k == 7))
                    seq.c("tensor", qmm)

                    def qev(e, s=s, half=half):
                        e.activation(out=qF[:, s * CAP + half * 512:][:, 0:512],
                                     in_=psQ[:], func=AF.Silu)
                    seq.c("scalar", qev)

            for ch in range(NCH):
                def kproj(e, ch=ch):
                    xh_ = xeFa if ch < 8 else xeFb
                    c2 = ch % 8
                    for half in range(2):
                        for k in range(8):
                            e.matmul(psP2[0:64, _sl(half, 512)],
                                     xh_[:, k * 512 + c2 * 64:][:, 0:64],
                                     wkt[:, k * D_MODEL + half * 512:][:, 0:512],
                                     start=(k == 0), stop=(k == 7))
                seq.c("tensor", kproj)

                def kev(e):
                    for half in range(2):
                        e.activation(out=ktC[:, _sl(half, 512)],
                                     in_=psP2[0:64, _sl(half, 512)], func=AF.Silu)
                seq.c("scalar", kev)

                def vproj(e, ch=ch):
                    xh_ = xeFa if ch < 8 else xeFb
                    c2 = ch % 8
                    for half in range(2):
                        for k in range(8):
                            e.matmul(psP2[0:64, _sl(half, 512)],
                                     xh_[:, k * 512 + c2 * 64:][:, 0:64],
                                     wvt4[:, k * D_MODEL + half * 512:][:, 0:512],
                                     start=(k == 0), stop=(k == 7))
                seq.c("tensor", vproj)

                def vev(e):
                    for half in range(2):
                        e.copy(out=vtC[:, _sl(half, 512)],
                               in_=psP2[0:64, _sl(half, 512)])
                seq.c("scalar", vev)

                def fproj(e, ch=ch):
                    xh_ = xeFa if ch < 8 else xeFb
                    c2 = ch % 8
                    for k in range(8):
                        e.matmul(psP2[0:64, 0:256],
                                 xh_[:, k * 512 + c2 * 64:][:, 0:64],
                                 wft[:, _sl(k, 256)],
                                 start=(k == 0), stop=(k == 7))
                seq.c("tensor", fproj)

                def fev(e):
                    e.copy(out=fe[:], in_=psP2[0:64, 0:256])
                seq.c("scalar", fev)

                def sp1(e):
                    e.tensor_scalar(out=nfe[:], in0=fe[:], scalar1=-1.0,
                                    scalar2=None, op0=OP.mult, op1=OP.bypass)
                    e.edge()
                    e.tensor_tensor(out=ab4[:], in0=fe[:], in1=nfe[:], op=OP.min)
                    e.edge()
                    e.tensor_scalar(out=fe[:], in0=nfe[:], scalar1=0.0,
                                    scalar2=None, op0=OP.max, op1=OP.bypass)
                seq.c("vector", sp1)

                def sp2(e):
                    e.activation(out=ab4[:], in_=ab4[:], func=AF.Exp)
                    e.edge()
                    e.activation(out=ab4[:], in_=ab4[:], func=AF.Ln, bias=1.0)
                seq.c("scalar", sp2)

                def sp3(e):
                    e.tensor_tensor(out=gP[:], in0=fe[:], in1=ab4[:], op=OP.add)
                    e.edge()
                    e.tensor_scalar(out=gP[:], in0=gP[:],
                                    scalar1=-1.0 / GATE_NORM, scalar2=None,
                                    op0=OP.mult, op1=OP.bypass)
                    e.edge()
                    e.tensor_copy(out=gPb[:], in_=gP[:])
                seq.c("vector", sp3)

                def sp4(e):
                    e.activation(out=nfe[:], in_=gP[:], func=AF.Exp)
                seq.c("scalar", sp4)

                def sp5(e):
                    e.tensor_scalar(out=stP[:], in0=nfe[:], scalar1=-1.0,
                                    scalar2=1.0, op0=OP.mult, op1=OP.add)
                    e.edge()
                    e.tensor_copy(out=stPb[:], in_=stP[:])
                seq.c("vector", sp5)

                def c1(e):
                    e.matmul(psG[:], trit[:], gPb[:], start=True, stop=True)
                    for h in range(4):
                        e.matmul(psG2[:, _sl(h, 64)], gPb[:, _sl(h, 64)], trit[:],
                                 start=True, stop=True)
                    for h in range(4):
                        e.transpose(psT4[0:64, _sl(h, 64)], stPb[:, _sl(h, 64)],
                                    idt4[0:64, 0:64])
                seq.c("tensor", c1)

                def c2(e):
                    e.activation(out=dec4[:], in_=psG[:], func=AF.Exp)
                    e.activation(out=exm[:], in_=psG[:], func=AF.Exp,
                                 scale=-1.0)
                    e.activation(out=exmT[:], in_=psG2[:], func=AF.Exp,
                                 scale=-1.0)
                    e.copy(out=GcT4[:], in_=psG2[:])
                    e.copy(out=stT4[:], in_=psT4[0:64, 0:256])
                    e.edge()
                    for h in range(4):
                        e.activation(out=glT4[:, h:h + 1],
                                     in_=GcT4[:, h * 64 + 63:h * 64 + 64],
                                     func=AF.Exp)
                seq.c("scalar", c2)

                def c3(e):
                    e.tensor_tensor(out=sin4[:], in0=stP[:], in1=exm[:],
                                    op=OP.mult)
                    e.tensor_tensor(out=sinT4[:], in0=stT4[:], in1=exmT[:],
                                    op=OP.mult)
                    e.tensor_copy(out=hkcast[:], in_=hkm[:])
                    e.tensor_copy(out=hvcast[:], in_=hvm[:])
                seq.c("vector", c3)

                def c4(e):
                    for h in range(4):
                        for b2 in range(2):
                            e.transpose(psT4[:, _sl(h * 2 + b2, 64)],
                                        ktC[:, h * 256 + b2 * 128:][:, 0:128],
                                        idt4[0:64, 0:64])
                    for h in range(4):
                        for b2 in range(2):
                            e.transpose(psT4[:, 512 + (h * 2 + b2) * 64:][:, 0:64],
                                        hkcast[:, h * 256 + b2 * 128:][:, 0:128],
                                        idt4[0:64, 0:64])
                seq.c("tensor", c4)

                def c5(e):
                    e.copy(out=kF8[:], in_=psT4[:, 0:512])
                    e.copy(out=hkT8[:], in_=psT4[:, 512:1024])
                seq.c("scalar", c5)

                def c6(e, ch=ch):
                    for h in range(4):
                        for b2 in range(2):
                            e.matmul(psOK[:, _sl(h, 64)],
                                     kF8[:, _sl(h * 2 + b2, 64)],
                                     qF[:, (2 * h + b2) * CAP + ch * 64:][:, 0:64],
                                     start=(b2 == 0), stop=(b2 == 1))
                seq.c("tensor", c6)

                def c7(e):
                    e.tensor_tensor(out=At4[:], in0=psOK[:], in1=trif[:],
                                    op=OP.mult)
                seq.c("vector", c7)

                def c8(e, ch=ch):
                    for h in range(4):
                        e.matmul(psOK[:, _sl(h, 64)], At4[:, _sl(h, 64)],
                                 sin4[:, _sl(h, 64)], start=True, stop=False)
                        for b2 in range(2):
                            e.matmul(psOK[:, _sl(h, 64)],
                                     qF[:, (2 * h + b2) * CAP + ch * 64:][:, 0:64],
                                     hkT8[:, _sl(h * 2 + b2, 64)],
                                     start=False, stop=(b2 == 1))
                seq.c("tensor", c8)

                def c9(e):
                    e.tensor_tensor(out=ok4[:], in0=dec4[:], in1=psOK[:],
                                    op=OP.mult)
                    e.edge()
                    e.tensor_reduce(
                        out=mx4[:].rearrange("p (h o) -> p h o", h=4),
                        in_=ok4[:].rearrange("p (h m) -> p h m", h=4),
                        axis=AX.X, op=OP.max)
                    e.edge()
                    for h in range(4):
                        e.tensor_scalar(out=qve[:, _sl(h, 64)],
                                        in0=ok4[:, _sl(h, 64)],
                                        scalar1=mx4[:, h:h + 1], scalar2=None,
                                        op0=OP.subtract, op1=OP.bypass)
                seq.c("vector", c9)

                def c10(e):
                    e.activation(out=qv32[:], in_=qve[:], func=AF.Exp)
                seq.c("scalar", c10)

                def c11(e):
                    e.tensor_reduce(
                        out=sm4[:].rearrange("p (h o) -> p h o", h=4),
                        in_=qv32[:].rearrange("p (h m) -> p h m", h=4),
                        axis=AX.X, op=OP.add)
                    e.edge()
                    e.reciprocal(out=sm4r[:], in_=sm4[:])
                    e.edge()
                    for h in range(4):
                        e.scalar_tensor_tensor(out=qv4[:, _sl(h, 64)],
                                               in0=qv32[:, _sl(h, 64)],
                                               scalar=sm4r[:, h:h + 1],
                                               in1=dec4[:, _sl(h, 64)],
                                               op0=OP.mult, op1=OP.mult)
                seq.c("vector", c11)

                def c12(e):
                    for h in range(4):
                        e.transpose(psT4[0:64, _sl(h, 64)], qv4[:, _sl(h, 64)],
                                    idt4[0:64, 0:64])
                seq.c("tensor", c12)

                def c13(e):
                    e.copy(out=qvT4[:], in_=psT4[0:64, 0:256])
                seq.c("scalar", c13)

                def c14(e):
                    for h in range(4):
                        e.matmul(psG2[:, _sl(h, 64)], sinT4[:, _sl(h, 64)],
                                 qvT4[:, _sl(h, 64)], start=True, stop=True)
                seq.c("tensor", c14)

                def c15(e):
                    e.tensor_tensor(out=TmT4[:], in0=psG2[:], in1=trif[:],
                                    op=OP.mult)
                seq.c("vector", c15)

                for h in range(4):
                    def c16(e, h=h):
                        e.matmul(psOK[:], TmT4[:, _sl(h, 64)],
                                 vtC[:, _sl(h, 256)], start=True, stop=False)
                        e.matmul(psOK[:], qvT4[:, _sl(h, 64)],
                                 hvcast[:, _sl(h, 256)], start=False, stop=True)
                    seq.c("tensor", c16)

                    def c16e(e, h=h):
                        e.copy(out=ov4[:, _sl(h, 256)], in_=psOK[:])
                    seq.c("scalar", c16e)

                def c17(e):
                    for h in range(4):
                        e.tensor_scalar(out=scrR[:, _sl(h, 64)],
                                        in0=GcT4[:, _sl(h, 64)],
                                        scalar1=GcT4[:, h * 64 + 63:h * 64 + 64],
                                        scalar2=None,
                                        op0=OP.subtract, op1=OP.bypass)
                seq.c("vector", c17)

                def c18(e):
                    e.activation(out=exmR[:], in_=scrR[:], func=AF.Exp,
                                 scale=-1.0)
                seq.c("scalar", c18)

                def c19(e):
                    e.tensor_tensor(out=srmT4[:], in0=stT4[:], in1=exmR[:],
                                    op=OP.mult)
                seq.c("vector", c19)

                def c20(e):
                    for h in range(4):
                        e.transpose(psT4[0:64, _sl(h, 64)], srmT4[:, _sl(h, 64)],
                                    idt4[0:64, 0:64])
                seq.c("tensor", c20)

                def c21(e):
                    e.copy(out=srm4[:], in_=psT4[0:64, 0:256])
                seq.c("scalar", c21)

                for base, dst in ((0, hkm), (1024, hvm)):
                    src = ktC if base == 0 else vtC
                    for hp in range(2):
                        def c22(e, src=src, hp=hp):
                            for j in range(2):
                                h = hp * 2 + j
                                ps = psU if j == 0 else psOK
                                e.matmul(ps[:], srm4[:, _sl(h, 64)],
                                         src[:, _sl(h, 256)],
                                         start=True, stop=True)
                        seq.c("tensor", c22)

                        def c22v(e, dst=dst, hp=hp):
                            for j in range(2):
                                h = hp * 2 + j
                                ps = psU if j == 0 else psOK
                                e.scalar_tensor_tensor(
                                    out=dst[:, _sl(h, 256)],
                                    in0=dst[:, _sl(h, 256)],
                                    scalar=glT4[:, h:h + 1], in1=ps[:],
                                    op0=OP.mult, op1=OP.add)
                        seq.c("vector", c22v)

                seq.d(lambda e, gi=gi, ch=ch: [e.dma_start(
                    out=ob_out[gi, _sl(ch, 64), :], in_=ov4[:])])

        seq.d(lambda e: [e.dma_start(out=tot_out, in_=tott[:])])
        if TRUNC4 is not None:
            seq.items = seq.items[:TRUNC4]
        seq.emit(block)
    return nc



def build_ph5():
    nc = Bacc("TRN2", target_bir_lowering=False)
    x1_in = nc.dram_tensor("x15", [TOK, D_MODEL], F32, kind="ExternalInput").ap()
    srw_in = nc.dram_tensor("srw5", [TOK, 4], F32, kind="ExternalInput").ap()
    yl_in = nc.dram_tensor("yl5", [D_MODEL, TOK], BF16, kind="ExternalInput").ap()
    cp_in = nc.dram_tensor("cp5", [D_MODEL, TOK], BF16, kind="ExternalInput").ap()
    lst_in = nc.dram_tensor("lst5", [4, 128, 16], F32, kind="ExternalInput").ap()
    qsel_in = nc.dram_tensor("qsel", [128, 4], F32, kind="ExternalInput").ap()
    wol_in = nc.dram_tensor("wol5", [128, 8 * D_MODEL], F32, kind="ExternalInput").ap()
    wo_in = nc.dram_tensor("wo5", [128, 8 * D_MODEL], F32, kind="ExternalInput").ap()
    gnw_in = nc.dram_tensor("gnw5", [128, 8], F32, kind="ExternalInput").ap()
    o5_in = nc.dram_tensor("o5", [8 * CAP, D_MODEL], BF16, kind="ExternalInput").ap()
    rk_in = nc.dram_tensor("rk5", [TOK, 8], F32, kind="ExternalInput").ap()
    tot_in = nc.dram_tensor("tot5", [128, 8], F32, kind="ExternalInput").ap()
    id_in = nc.dram_tensor("ident", [128, 128], BF16, kind="ExternalInput").ap()
    cv_in = nc.dram_tensor("cvec", [128, 8], F32, kind="ExternalInput").ap()

    x2_out = nc.dram_tensor("x2", [TOK, D_MODEL], F32, kind="ExternalOutput").ap()
    gidx_h = nc.dram_tensor("gidx_h", [1, 2 * PT3], I16, kind="Internal").ap()

    with ExitStack() as st:
        ex = st.enter_context
        wolt = ex(nc.sbuf_tensor("wolt", [128, 8 * D_MODEL], BF16))
        wot5 = ex(nc.sbuf_tensor("wot5", [128, 8 * D_MODEL], BF16))
        wstg5 = ex(nc.sbuf_tensor("wstg5", [128, 4096], F32))
        gnwt = ex(nc.sbuf_tensor("gnwt", [128, 8], F32))
        cvt5 = ex(nc.sbuf_tensor("cvt5", [128, 8], F32))
        iot5 = ex(nc.sbuf_tensor("iot5", [128, 8], F32))
        idt5 = ex(nc.sbuf_tensor("idt5", [128, 128], BF16))
        qselt = ex(nc.sbuf_tensor("qselt", [128, 4], F32))
        tot5t = ex(nc.sbuf_tensor("tot5t", [128, 8], F32))
        lstt5 = ex(nc.sbuf_tensor("lstt5", [128, 4 * 16], F32))
        h0t = ex(nc.sbuf_tensor("h0t", [128, 8], F32))
        pbuf = ex(nc.sbuf_tensor("pbuf", [128, 8], F32))
        tbuf = ex(nc.sbuf_tensor("tbuf", [128, 8], F32))

        ylt5 = ex(nc.sbuf_tensor("ylt5", [128, 8 * 512], BF16))
        cpt5 = ex(nc.sbuf_tensor("cpt5", [128, 8 * 512], BF16))
        yst = ex(nc.sbuf_tensor("yst", [128, 8 * 512], BF16))
        s1t = ex(nc.sbuf_tensor("s1t", [128, 4 * D_MODEL], BF16))
        x1t5 = ex(nc.sbuf_tensor("x1t5", [128, 4 * D_MODEL], F32))
        x2t = ex(nc.sbuf_tensor("x2t", [128, 4 * D_MODEL], F32))
        srwt5 = ex(nc.sbuf_tensor("srwt5", [128, 4 * 4], F32))
        rkt5 = ex(nc.sbuf_tensor("rkt5", [128, 4 * 8], F32))
        m8a5 = ex(nc.sbuf_tensor("m8a5", [128, 8], F32))
        m8b5 = ex(nc.sbuf_tensor("m8b5", [128, 8], F32))
        scr5 = ex(nc.sbuf_tensor("scr5", [128, 8], F32))
        cw5 = ex(nc.sbuf_tensor("cw5", [128, 8], F32))
        rwm5 = ex(nc.sbuf_tensor("rwm5", [128, 4 * 2], F32))
        gi8 = ex(nc.sbuf_tensor("gi8", [128, 8], F32))
        gi16 = ex(nc.sbuf_tensor("gi16", [128, 8], I16))
        widxw5 = ex(nc.sbuf_tensor("widxw5", [128, 64], I16))
        oga0 = ex(nc.sbuf_tensor("oga0", [128, 8 * 512], BF16))
        oga1 = ex(nc.sbuf_tensor("oga1", [128, 8 * 512], BF16))
        ost0 = ex(nc.sbuf_tensor("ost0", [128, 1024], F32))
        ost1 = ex(nc.sbuf_tensor("ost1", [128, 1024], F32))
        mot = ex(nc.sbuf_tensor("mot", [128, 1024], F32))
        sq5 = ex(nc.sbuf_tensor("sq5", [128, 1024], F32))
        stt5 = ex(nc.sbuf_tensor("stt5", [128, 8], F32))
        xot = ex(nc.sbuf_tensor("xot", [128, 1024], BF16))
        xoF = ex(nc.sbuf_tensor("xoF", [128, 1024], BF16))
        g1t = ex(nc.sbuf_tensor("g1t", [128, 1024], BF16))

        psS = ex(nc.psum_tensor("psS", [128, 512], F32))
        psT5 = ex(nc.psum_tensor("psT5", [128, 1024], BF16))

        block = ex(nc.Block())
        seq = Seq(nc, ex)

        seq.g(lambda e: [e.dma_start(out=wolt[:], in_=wol_in),
                         e.dma_start(out=idt5[:], in_=id_in)], 2)
        seq.d(lambda e: [e.dma_start(out=gnwt[:], in_=gnw_in),
                         e.dma_start(out=cvt5[:], in_=cv_in),
                         e.dma_start(out=qselt[:], in_=qsel_in),
                         e.dma_start(out=tot5t[:], in_=tot_in),
                         e.dma_start(out=lstt5[:].rearrange("p (a c) -> p a c", a=4),
                                     in_=lst_in.rearrange("a p c -> p a c"))], 5)

        for half in range(2):
            seq.d(lambda e, half=half: [e.dma_start(
                out=wstg5[:], in_=wo_in[:, half * 4096:(half + 1) * 4096])])

            def fw5(e, half=half):
                for k in range(4):
                    kk = half * 4 + k
                    e.tensor_scalar(out=wot5[:, _sl(kk, D_MODEL)],
                                    in0=wstg5[:, _sl(k, D_MODEL)],
                                    scalar1=gnwt[:, kk:kk + 1], scalar2=None,
                                    op0=OP.mult, op1=OP.bypass)
            seq.c("vector", fw5)

        def prep5(e):
            for k in range(8):
                e.memset(iot5[:, k:k + 1], float(k))
            e.tensor_copy(out=pbuf[:], in_=lstt5[:, 0:8])
            e.edge()
            e.tensor_scalar(out=h0t[:], in0=pbuf[:], scalar1=qselt[:, 1:2],
                            scalar2=None, op0=OP.mult, op1=OP.bypass)
            for j in (1, 2):
                e.tensor_tensor(out=tbuf[:], in0=lstt5[:, j * 16 + 8:j * 16 + 16],
                                in1=pbuf[:], op=OP.mult)
                e.edge()
                e.tensor_tensor(out=pbuf[:], in0=tbuf[:],
                                in1=lstt5[:, j * 16:j * 16 + 8], op=OP.add)
                e.edge()
                e.scalar_tensor_tensor(out=h0t[:], in0=pbuf[:],
                                       scalar=qselt[:, j + 1:j + 2], in1=h0t[:],
                                       op0=OP.mult, op1=OP.add)
                e.edge()
        seq.c("vector", prep5)

        for ti in range(NT3):
            seq.d(lambda e, ti=ti: [
                e.dma_start(out=ylt5[:].rearrange("p (a t) -> p a t", a=8),
                            in_=yl_in.rearrange("(a p) t -> p a t", p=128)[:, :, _sl(ti, PT3)]),
                e.dma_start(out=cpt5[:].rearrange("p (a t) -> p a t", a=8),
                            in_=cp_in.rearrange("(a p) t -> p a t", p=128)[:, :, _sl(ti, PT3)]),
                e.dma_start(out=x1t5[:].rearrange("p (a d) -> p a d", a=4),
                            in_=x1_in[_sl(ti, PT3), :].rearrange("(a p) d -> p a d", p=128)),
                e.dma_start(out=srwt5[:].rearrange("p (a c) -> p a c", a=4),
                            in_=srw_in[_sl(ti, PT3), :].rearrange("(a p) c -> p a c", p=128)),
                e.dma_start(out=rkt5[:].rearrange("p (a c) -> p a c", a=4),
                            in_=rk_in[_sl(ti, PT3), :].rearrange("(a p) c -> p a c", p=128)),
            ], 5)

            def yco(e):
                for nt in range(8):
                    e.scalar_tensor_tensor(out=yst[:, _sl(nt, PT3)],
                                           in0=cpt5[:, _sl(nt, PT3)],
                                           scalar=h0t[:, nt:nt + 1],
                                           in1=ylt5[:, _sl(nt, PT3)],
                                           op0=OP.mult, op1=OP.add)
            seq.c("vector", yco)

            def ysi(e):
                for nt in range(8):
                    e.activation(out=yst[:, _sl(nt, PT3)], in_=yst[:, _sl(nt, PT3)],
                                 func=AF.Silu)
            seq.c("scalar", ysi)

            for tc in range(4):
                for half in range(2):
                    def smm(e, tc=tc, half=half):
                        for k in range(8):
                            e.matmul(psS[:], yst[:, k * PT3 + tc * 128:][:, 0:128],
                                     wolt[:, k * D_MODEL + half * 512:][:, 0:512],
                                     start=(k == 0), stop=(k == 7))
                    seq.c("tensor", smm)

                    def sev(e, tc=tc, half=half):
                        e.copy(out=s1t[:, tc * D_MODEL + half * 512:][:, 0:512],
                               in_=psS[:])
                    seq.c("scalar", sev)

            for grp in range(4):
                def rt5(e, grp=grp):
                    e.tensor_scalar(out=m8a5[:], in0=iot5[:],
                                    scalar1=srwt5[:, grp * 4:grp * 4 + 1],
                                    scalar2=None, op0=OP.is_equal, op1=OP.bypass)
                    e.tensor_scalar(out=m8b5[:], in0=iot5[:],
                                    scalar1=srwt5[:, grp * 4 + 1:grp * 4 + 2],
                                    scalar2=None, op0=OP.is_equal, op1=OP.bypass)
                    e.edge()
                    for sk, msk in ((0, m8a5), (1, m8b5)):
                        e.tensor_tensor(out=scr5[:], in0=msk[:],
                                        in1=rkt5[:, _sl(grp, 8)], op=OP.mult)
                        e.edge()
                        e.tensor_reduce(out=cw5[:, 0:1], in_=scr5[:], axis=AX.X,
                                        op=OP.add)
                        e.tensor_tensor(out=scr5[:], in0=msk[:], in1=tot5t[:],
                                        op=OP.mult)
                        e.edge()
                        e.tensor_reduce(out=cw5[:, 1:2], in_=scr5[:], axis=AX.X,
                                        op=OP.add)
                        e.edge()
                        e.tensor_scalar(out=cw5[:, 2:3], in0=cw5[:, 0:1],
                                        scalar1=cw5[:, 1:2], scalar2=float(CAP - 1),
                                        op0=OP.subtract, op1=OP.add)
                        e.edge()
                        e.tensor_scalar(out=cw5[:, 3:4], in0=cw5[:, 2:3],
                                        scalar1=0.0, scalar2=None,
                                        op0=OP.is_ge, op1=OP.bypass)
                        e.edge()
                        e.tensor_tensor(out=rwm5[:, grp * 2 + sk:grp * 2 + sk + 1],
                                        in0=cw5[:, 3:4],
                                        in1=srwt5[:, grp * 4 + 2 + sk:grp * 4 + 3 + sk],
                                        op=OP.mult)
                        e.tensor_tensor(out=cw5[:, 4:5], in0=cw5[:, 2:3],
                                        in1=cw5[:, 3:4], op=OP.mult)
                        e.tensor_scalar(out=cw5[:, 5:6],
                                        in0=srwt5[:, grp * 4 + sk:grp * 4 + sk + 1],
                                        scalar1=float(CAP), scalar2=None,
                                        op0=OP.mult, op1=OP.bypass)
                        e.edge()
                        e.tensor_tensor(out=gi8[:, sk * 4 + grp:sk * 4 + grp + 1],
                                        in0=cw5[:, 4:5], in1=cw5[:, 5:6], op=OP.add)
                seq.c("vector", rt5)

            def gcv(e):
                e.tensor_copy(out=gi16[:], in_=gi8[:])
            seq.c("vector", gcv)

            seq.d(lambda e: [e.dma_start(
                out=gidx_h[0].rearrange("(p c) -> p c", p=128), in_=gi16[:])])
            seq.c("scalar", lambda e: e.nop())
            seq.d(lambda e: [
                e.dma_start(out=widxw5[16 * k:16 * (k + 1), :]
                            .rearrange("p (s g c) -> p s g c", s=2, g=4),
                            in_=gidx_h[0].rearrange("(c e s g) -> e s g c",
                                                    c=8, e=16, s=2),
                            allow_slow_non_contiguous=True)
                for k in range(8)], 8)

            def gat5(e):
                return [
                    e.dma_gather(
                        out_ap=oga0[:].rearrange("p (a t) -> p a t", a=8),
                        in_ap=o5_in, idxs_ap=widxw5[:, 0:32], num_idxs=PT3,
                        num_idxs_reg=PT3, elem_size=D_MODEL, transpose=True),
                    e.dma_gather(
                        out_ap=oga1[:].rearrange("p (a t) -> p a t", a=8),
                        in_ap=o5_in, idxs_ap=widxw5[:, 32:64], num_idxs=PT3,
                        num_idxs_reg=PT3, elem_size=D_MODEL, transpose=True),
                ]
            seq.g(gat5, 2)

            for grp in range(4):
                for sk, ost in ((0, ost0), (1, ost1)):
                    def tr5(e, grp=grp, sk=sk):
                        og = oga0 if sk == 0 else oga1
                        for dt in range(8):
                            e.transpose(psT5[:, _sl(dt, 128)],
                                        og[:, dt * 512 + grp * 128:][:, 0:128],
                                        idt5[:])
                    seq.c("tensor", tr5)

                    def te5(e, ost=ost):
                        e.copy(out=ost[:], in_=psT5[:])
                    seq.c("scalar", te5)

                def cmb(e, grp=grp):
                    e.tensor_scalar(out=mot[:], in0=ost0[:],
                                    scalar1=rwm5[:, grp * 2:grp * 2 + 1],
                                    scalar2=None, op0=OP.mult, op1=OP.bypass)
                    e.edge()
                    e.scalar_tensor_tensor(out=mot[:], in0=ost1[:],
                                           scalar=rwm5[:, grp * 2 + 1:grp * 2 + 2],
                                           in1=mot[:], op0=OP.mult, op1=OP.add)
                seq.c("vector", cmb)

                def rm5(e):
                    e.activation(out=sq5[:], in_=mot[:], func=AF.Square,
                                 accum_out=stt5[:, 0:1])
                    e.edge()
                    e.activation(out=stt5[:, 1:2], in_=stt5[:, 0:1], func=AF.Sqrt,
                                 scale=cvt5[:, 0:1], bias=cvt5[:, 1:2])
                seq.c("scalar", rm5)

                def rx5(e):
                    e.reciprocal(out=stt5[:, 2:3], in_=stt5[:, 1:2])
                    e.edge()
                    e.tensor_scalar(out=xot[:], in0=mot[:], scalar1=stt5[:, 2:3],
                                    scalar2=None, op0=OP.mult, op1=OP.bypass)
                seq.c("vector", rx5)

                def tx5(e):
                    for dt in range(8):
                        e.transpose(psT5[:, _sl(dt, 128)], xot[:, _sl(dt, 128)],
                                    idt5[:])
                seq.c("tensor", tx5)

                def xe5(e):
                    e.copy(out=xoF[:], in_=psT5[:])
                seq.c("scalar", xe5)

                for half in range(2):
                    def omm(e, half=half):
                        for k in range(8):
                            e.matmul(psS[:], xoF[:, _sl(k, 128)],
                                     wot5[:, k * D_MODEL + half * 512:][:, 0:512],
                                     start=(k == 0), stop=(k == 7))
                    seq.c("tensor", omm)

                    def oev(e, half=half):
                        e.activation(out=g1t[:, _sl(half, 512)], in_=psS[:],
                                     func=AF.Silu)
                    seq.c("scalar", oev)

                def fx5(e, grp=grp):
                    e.tensor_tensor(out=x2t[:, _sl(grp, D_MODEL)],
                                    in0=s1t[:, _sl(grp, D_MODEL)], in1=g1t[:],
                                    op=OP.mult)
                    e.edge()
                    e.tensor_tensor(out=x2t[:, _sl(grp, D_MODEL)],
                                    in0=x2t[:, _sl(grp, D_MODEL)],
                                    in1=x1t5[:, _sl(grp, D_MODEL)], op=OP.add)
                seq.c("vector", fx5)

            seq.d(lambda e, ti=ti: [e.dma_start(
                out=x2_out[_sl(ti, PT3), :].rearrange("(a p) d -> p a d", p=128),
                in_=x2t[:].rearrange("p (a d) -> p a d", a=4))])

        seq.emit(block)
    return nc



def _rep(v):
    return np.ascontiguousarray(np.broadcast_to(np.asarray(v)[None, :], (128, len(v))))


def _ktiles(wT, k):
    K, N = wT.shape
    return np.ascontiguousarray(wT.reshape(K // 128, 128, N).transpose(1, 0, 2)
                                .reshape(128, (K // 128) * N))


def _cols128(v):
    return np.ascontiguousarray(np.asarray(v).reshape(-1, 128).T)


def ph1_inputs(inputs):
    x = np.asarray(inputs["L0_attn_x"])
    ident = np.eye(128, dtype=np.float32)
    import ml_dtypes
    ident = ident.astype(ml_dtypes.bfloat16)

    win_t = _ktiles(np.ascontiguousarray(np.asarray(inputs["L0_attn_in_proj_w"]).T), 8)
    conv_w = np.stack([np.asarray(inputs["L0_attn_conv_w_f"])[:, 0, :],
                       np.asarray(inputs["L0_attn_conv_w_r"])[:, 0, :]])
    cw = conv_w.reshape(2, 16, 128, 4).transpose(2, 0, 1, 3).reshape(128, 2 * 16 * 4)
    cb = np.stack([np.asarray(inputs["L0_attn_conv_b_f"]),
                   np.asarray(inputs["L0_attn_conv_b_r"])])
    cb = cb.reshape(2, 16, 128).transpose(2, 0, 1).reshape(128, 32)
    xpw = np.concatenate([np.asarray(inputs["L0_attn_x_proj_w_f"]).T,
                          np.asarray(inputs["L0_attn_x_proj_w_r"]).T], axis=1)
    xpw_t = _ktiles(xpw, 16)
    w1 = _ktiles(np.asarray(inputs["L0_ffn_w1"]), 8)
    w2 = _ktiles(np.asarray(inputs["L0_ffn_w2"]), 32)

    base = dict(
        norm_w=_rep(np.asarray(inputs["L0_attn_norm_w"])).astype(np.float32),
        norm_b=_rep(np.asarray(inputs["L0_attn_norm_b"])).astype(np.float32),
        win_t=win_t.astype(np.float32),
        conv_w=cw.astype(np.float32), conv_b=cb.astype(np.float32),
        xpw_t=xpw_t.astype(np.float32),
        w1=w1.astype(np.float32), b1=_cols128(inputs["L0_ffn_b1"]).astype(np.float32),
        w2=w2.astype(np.float32), b2=_cols128(inputs["L0_ffn_b2"]).astype(np.float32),
        ident=ident,
        cvec=np.broadcast_to(np.array([1.0 / D_MODEL, 1e-5, -1.0, ISQ2, 256.0,
                                       1.0 / 256, -TWO23 / 256, 1.0],
                                      np.float32)[None, :], (128, 8)).copy(),
    )
    maps = []
    for c in range(8):
        b, q = c // 4, c % 4
        lo, hi = q * TOK - HALO, (q + 1) * TOK + HALO
        xh = np.zeros((HTOK, D_MODEL), np.float32)
        s_lo, s_hi = max(lo, 0), min(hi, SEQ)
        xh[s_lo - lo:s_hi - lo] = x[b, s_lo:s_hi]
        m = dict(base)
        m["x_halo"] = xh
        maps.append(m)
    return maps


def ph2_inputs(inputs, r1):
    import ml_dtypes
    uf = {b: np.concatenate([r1[b * 4 + q]["u_f"] for q in range(4)], axis=1)
          for b in range(2)}
    ur = {b: np.concatenate([r1[b * 4 + q]["u_r"] for q in range(4)], axis=1)
          for b in range(2)}
    xdb = {b: np.concatenate([r1[b * 4 + q]["xdbl"] for q in range(4)], axis=1)
           for b in range(2)}
    sel = np.zeros((32, 32, 128), np.float32)
    for m in range(32):
        sel[m, m, :] = 1.0
    sel = sel.reshape(32, 32 * 128).astype(ml_dtypes.bfloat16)

    def dtw_pair(dq):
        lo = dq * 512
        return np.stack([
            np.ascontiguousarray(np.asarray(inputs["L0_attn_dt_proj_w_f"])[lo:lo + 512, :].T),
            np.ascontiguousarray(np.asarray(inputs["L0_attn_dt_proj_w_r"])[lo:lo + 512, :].T),
        ]).astype(np.float32)

    def col4(v, dq):
        return np.ascontiguousarray(np.asarray(v)[dq * 512:(dq + 1) * 512]
                                    .reshape(4, 128).T).astype(np.float32)

    def al64(v, dq):
        a = np.asarray(v)[dq * 512:(dq + 1) * 512].reshape(4, 128, 16)
        return np.ascontiguousarray(a.transpose(1, 0, 2).reshape(128, 64)).astype(np.float32)

    maps = []
    for c in range(8):
        b, dq = c // 4, c % 4
        lo = dq * 512
        u2 = np.stack([
            uf[b][lo:lo + 512].reshape(4, 128, SEQ),
            ur[b][lo:lo + 512, ::-1].reshape(4, 128, SEQ),
        ])
        xd2 = np.stack([xdb[b][0:96], xdb[b][96:192, ::-1]]).astype(np.float32)
        maps.append(dict(
            u2=np.ascontiguousarray(u2),
            xdbl2=np.ascontiguousarray(xd2),
            dtw=dtw_pair(dq),
            dtb=np.stack([col4(inputs["L0_attn_dt_proj_b_f"], dq),
                          col4(inputs["L0_attn_dt_proj_b_r"], dq)]),
            alog=np.stack([al64(inputs["L0_attn_A_log_f"], dq),
                           al64(inputs["L0_attn_A_log_r"], dq)]),
            sel32=sel,
        ))
    return maps


def ph3_inputs(inputs, r1, r2):
    x = np.asarray(inputs["L0_attn_x"])
    yfull = {}
    for b in range(2):
        yf = np.concatenate([r2[b * 4 + dq]["ysc"][0].reshape(512, SEQ)
                             for dq in range(4)], axis=0)
        yr = np.concatenate([r2[b * 4 + dq]["ysc"][1].reshape(512, SEQ)[:, ::-1]
                             for dq in range(4)], axis=0)
        yfull[b] = (yf, yr)
    d2 = np.concatenate([
        np.asarray(inputs["L0_attn_D_f"]).reshape(16, 128).T,
        np.asarray(inputs["L0_attn_D_r"]).reshape(16, 128).T], axis=1).astype(np.float32)
    wout = _ktiles(np.ascontiguousarray(np.asarray(inputs["L0_attn_out_proj_w"]).T), 16)
    wfiv = np.concatenate([
        _ktiles(np.ascontiguousarray(np.asarray(inputs[k]).T), 8)
        for k in ("L1_attn_Wf", "L1_attn_Wi", "L1_attn_Wv")], axis=1)
    wg = _ktiles(np.ascontiguousarray(np.asarray(inputs["L1_ffn_Wg"]).T), 8)
    import ml_dtypes
    base = dict(
        D2=d2, wout=wout.astype(np.float32), wfiv=wfiv.astype(np.float32),
        wg=wg.astype(np.float32), nw2=_cols128(inputs["L1_ffn_norm_w"]).astype(np.float32),
        ident=np.eye(128, dtype=np.float32).astype(ml_dtypes.bfloat16),
        cvec=np.broadcast_to(np.array([1.0 / D_MODEL, 1e-5, -1.0, ISQ2, 256.0,
                                       1.0 / 256, -TWO23 / 256, 1.0],
                                      np.float32)[None, :], (128, 8)).copy(),
    )
    maps = []
    for c in range(8):
        b, q = c // 4, c % 4
        sl_ = slice(q * TOK, (q + 1) * TOK)
        m = dict(base)
        m["x3"] = np.ascontiguousarray(x[b, sl_]).astype(np.float32)
        m["z3"] = r1[c]["z"]
        m["uf3"] = r1[c]["u_f"]
        m["ur3"] = r1[c]["u_r"]
        m["gt3"] = np.ascontiguousarray(r1[c]["g"].T)
        m["yf3"] = np.ascontiguousarray(yfull[b][0][:, sl_])
        m["yr3"] = np.ascontiguousarray(yfull[b][1][:, sl_])
        maps.append(m)
    return maps


def ph4_inputs(inputs, r3):
    import ml_dtypes
    xh = {b: np.concatenate([r3[b * 4 + q]["xhat2"] for q in range(4)], axis=0)
          for b in range(2)}
    me = {b: np.concatenate([r3[b * 4 + q]["me8"] for q in range(4)], axis=0)
          for b in range(2)}
    Wk = np.asarray(inputs["L1_ffn_Wk"])
    Wv = np.asarray(inputs["L1_ffn_Wv"])
    Wf = np.asarray(inputs["L1_ffn_Wf"])
    triu = np.triu(np.ones((64, 64), np.float32))
    base = dict(
        wq4=_ktiles(np.ascontiguousarray(np.asarray(inputs["L1_ffn_Wq"]).T), 8)
        .astype(np.float32),
        nw4=_cols128(inputs["L1_ffn_norm_w"]).astype(np.float32),
        triu64=triu.astype(ml_dtypes.bfloat16),
        triuf4=np.ascontiguousarray(np.tile(triu, (1, 4))).astype(np.float32),
        ident=np.eye(128, dtype=np.float32).astype(ml_dtypes.bfloat16),
        rb16=(np.arange(16, dtype=np.float32) * 1024.0).reshape(16, 1),
    )
    maps = []
    for c in range(8):
        b, q = c // 4, c % 4
        es = [2 * q, 2 * q + 1]
        m = dict(base)
        m["xh4"] = xh[b]
        m["me2"] = np.ascontiguousarray(me[b][:, es].T).astype(np.float32)
        m["wk4"] = np.stack([_ktiles(np.ascontiguousarray(Wk[e].T), 8)
                             for e in es]).astype(np.float32)
        m["wv4"] = np.stack([_ktiles(np.ascontiguousarray(Wv[e].T), 8)
                             for e in es]).astype(np.float32)
        m["wf4"] = np.stack([_ktiles(np.ascontiguousarray(Wf[e].T), 8)
                             for e in es]).astype(np.float32)
        maps.append(m)
    return maps


def ph5_inputs(inputs, r3, r4):
    o_all = {b: np.concatenate([r4[b * 4 + qq]["oblk"][g]
                                for qq in range(4) for g in range(2)], axis=0)
             for b in range(2)}
    rkf = {b: np.concatenate([r4[b * 4 + qq]["rk2"] for qq in range(4)], axis=0)
           for b in range(2)}
    tots = {b: np.concatenate([r4[b * 4 + qq]["tot2"][0, 0:2] for qq in range(4)])
            for b in range(2)}
    import ml_dtypes
    base = dict(
        wol5=_ktiles(np.ascontiguousarray(np.asarray(inputs["L1_attn_Wo"]).T), 8)
        .astype(np.float32),
        wo5=_ktiles(np.ascontiguousarray(np.asarray(inputs["L1_ffn_Wo"]).T), 8)
        .astype(np.float32),
        gnw5=_cols128(inputs["L1_ffn_gnorm_w"]).astype(np.float32),
        ident=np.eye(128, dtype=np.float32).astype(ml_dtypes.bfloat16),
        cvec=np.broadcast_to(np.array([1.0 / D_MODEL, 1e-5, -1.0, ISQ2, 256.0,
                                       1.0 / 256, -TWO23 / 256, 1.0],
                                      np.float32)[None, :], (128, 8)).copy(),
    )
    maps = []
    for c in range(8):
        b, q = c // 4, c % 4
        qsel = np.zeros(4, np.float32)
        qsel[q] = 1.0
        m = dict(base)
        m["x15"] = r3[c]["x1"]
        m["srw5"] = r3[c]["selrw"]
        m["yl5"] = r3[c]["yloc"]
        m["cp5"] = r3[c]["cprod"]
        m["lst5"] = np.stack([r3[b * 4 + j]["lstates"] for j in range(4)])
        m["qsel"] = _rep(qsel).astype(np.float32)
        m["o5"] = o_all[b]
        m["rk5"] = np.ascontiguousarray(rkf[b][:, q * TOK:(q + 1) * TOK].T)
        m["tot5"] = _rep(tots[b]).astype(np.float32)
        maps.append(m)
    return maps


def kernel(**inputs):
    cores = list(range(8))
    nc1 = build_ph1()
    nc1.finalize()
    r1 = run_bass_kernel_spmd(nc1, ph1_inputs(inputs), core_ids=cores).results
    nc2 = build_ph2()
    nc2.finalize()
    r2 = run_bass_kernel_spmd(nc2, ph2_inputs(inputs, r1), core_ids=cores).results
    nc3 = build_ph3()
    nc3.finalize()
    r3 = run_bass_kernel_spmd(nc3, ph3_inputs(inputs, r1, r2), core_ids=cores).results
    nc4 = build_ph4()
    nc4.finalize()
    r4 = run_bass_kernel_spmd(nc4, ph4_inputs(inputs, r3), core_ids=cores).results
    nc5 = build_ph5()
    nc5.finalize()
    r5 = run_bass_kernel_spmd(nc5, ph5_inputs(inputs, r3, r4), core_ids=cores).results
    out = np.empty((BATCH, SEQ, D_MODEL), np.float32)
    for c in range(8):
        b, q = c // 4, c % 4
        out[b, q * TOK:(q + 1) * TOK] = r5[c]["x2"].astype(np.float32)
    return out
